# version 02 (final)
# baseline (speedup 1.0000x reference)
import math
from contextlib import ExitStack

import numpy as np

import concourse.bass as bass
import concourse.mybir as mybir
from concourse.bass_utils import run_bass_kernel_spmd

F32 = mybir.dt.float32
BF16 = mybir.dt.bfloat16
FP16 = mybir.dt.float16
I32 = mybir.dt.int32
AF = mybir.ActivationFunctionType
ALU = mybir.AluOpType
AX = mybir.AxisListType

S = 2048
D = 1024
H = 16
NE = 8
FF = 2816
DH = 4096
NT = S // 128
ND = D // 128
NTCH = S // 512
U1 = max(1, min(S, int(5.0 * math.log(S + 1))))
TWO_PI = 2.0 * math.pi
GRP = 8
NGRP = H // GRP
ECH_G = GRP * 64 // 128
MOE_TP = 1024
N_MOE_TP = S // MOE_TP
NFG = FF // 128
FFN_TP = 256
N_FFN_TP = S // FFN_TP
NCH = DH // 128
NTL = FFN_TP // 128

ENGINES = ("sync", "tensor", "vector", "scalar", "gpsimd")


SETTLE_SEMS = ("ve", "sc", "gp")


class Prog:
    def __init__(self, nc, waited=None):
        self.nc = nc
        self.q = {e: [] for e in ENGINES}
        self.sems = {}
        self.cnt = {}
        self.stack = ExitStack()
        self.record = waited is None
        self.waited = waited if waited is not None else {s: set()
                                                         for s in SETTLE_SEMS}
        self.real_ord = {s: 0 for s in SETTLE_SEMS}
        self.settle_fn = {}

    def new_sem(self, name):
        h = self.stack.enter_context(self.nc.semaphore(name))
        self.sems[name] = h
        self.cnt[name] = 0

    def emit(self, e, fn):
        self.q[e].append(fn)

    def wait(self, e, name, val):
        if not val:
            return
        if self.record and name in self.waited:
            self.waited[name].add(val)
        h = self.sems[name]
        self.emit(e, lambda eng, h=h, v=val: eng.wait_ge(h, v))
        if name == "te" and e in ("vector", "scalar") and e in self.settle_fn:
            sf = self.settle_fn[e]
            self.emit(e, lambda eng, sf=sf: sf(eng))

    def run(self, e, fn, inc=None, n=1):
        if inc is None:
            self.emit(e, fn)
            return None
        h = self.sems[inc]
        if inc in self.real_ord:
            self.real_ord[inc] += 1
            if (not self.record) and self.real_ord[inc] in self.waited[inc] \
                    and e in self.settle_fn:
                self.cnt[inc] += n
                self.emit(e, lambda eng, f=fn: f(eng))
                sf = self.settle_fn[e]
                self.emit(e, lambda eng, sf=sf: sf(eng))
                self.emit(e, lambda eng, sf=sf, h=h, n=n:
                          sf(eng).then_inc(h, n))
                return self.cnt[inc]
        self.cnt[inc] += n
        self.emit(e, lambda eng, f=fn, h=h, n=n: f(eng).then_inc(h, n))
        if inc == "ve" and e == "vector" and "vector" in self.settle_fn:
            sf = self.settle_fn["vector"]
            self.emit(e, lambda eng, sf=sf: sf(eng))
        return self.cnt[inc]

    def barrier(self):
        for e in ENGINES:
            if e in self.settle_fn:
                sf = self.settle_fn[e]
                self.emit(e, lambda eng, sf=sf: sf(eng))
                self.emit(e, lambda eng, sf=sf: sf(eng))
            self.cnt["b_" + e] += 1
            h = self.sems["b_" + e]
            self.emit(e, lambda eng, h=h: eng.sem_inc(h, 1))
        snap = {e: self.cnt["b_" + e] for e in ENGINES}
        dma_snap = {k: self.cnt[k] for k in self.sems if k.startswith("d_")}
        for e in ENGINES:
            for e2 in ENGINES:
                if e2 != e:
                    self.wait(e, "b_" + e2, snap[e2])
            for k, v in dma_snap.items():
                self.wait(e, k, v)

    def build_block(self):
        with self.nc.Block() as blk:
            @blk.sync
            def _(eng):
                for f in self.q["sync"]:
                    f(eng)

            @blk.tensor
            def _(eng):
                for f in self.q["tensor"]:
                    f(eng)

            @blk.vector
            def _(eng):
                for f in self.q["vector"]:
                    f(eng)

            @blk.scalar
            def _(eng):
                for f in self.q["scalar"]:
                    f(eng)

            @blk.gpsimd
            def _(eng):
                for f in self.q["gpsimd"]:
                    f(eng)


class Banks:
    def __init__(self, prog, handles):
        self.p = prog
        self.h = handles
        self.free_tok = [None] * len(handles)
        self.rr = 0

    def acquire(self, use_on="tensor"):
        i = self.rr
        self.rr = (self.rr + 1) % len(self.h)
        tok = self.free_tok[i]
        if tok is not None:
            self.p.wait(use_on, tok[0], tok[1])
            self.free_tok[i] = None
        return i

    def release(self, i, semname, val):
        self.free_tok[i] = (semname, val)


def ts(i, n):
    return slice(i * n, (i + 1) * n)


def build_nc(stage=9):
    nc, p = _build_nc(stage, None)
    return _build_nc(stage, p.waited)[0]


def _build_nc(stage=9, _waited=None):
    nc = bass.Bass("TRN2", target_bir_lowering=False,
                   detect_race_conditions=False)
    p = Prog(nc, _waited)
    st = p.stack
    V, W = p.run, p.wait
    _ctr = [0]

    def sbt(name, shape, dtype):
        _ctr[0] += 1
        return nc.sbuf_tensor(f"{name}_{_ctr[0]}", shape, dtype)

    xin = nc.dram_tensor("x", [S, D], F32, kind="ExternalInput").ap()
    w = {}
    for name, shape in [
        ("L0_attn_W_Q", [D, D]), ("L0_attn_W_K", [D, D]),
        ("L0_attn_W_V", [D, D]), ("L0_attn_W_O", [D, D]),
        ("L0_ffn_router_w", [D, NE]),
        ("L0_ffn_gate_proj", [NE, D, FF]), ("L0_ffn_up_proj", [NE, D, FF]),
        ("L0_ffn_down_proj", [NE, FF, D]),
        ("L1_attn_Wq", [D, D]), ("L1_attn_Wk", [D, D]),
        ("L1_attn_Wv", [D, D]), ("L1_attn_Wo", [D, D]),
        ("L1_attn_bq", [D]), ("L1_attn_bk", [D]),
        ("L1_attn_bv", [D]), ("L1_attn_bo", [D]),
        ("L1_ffn_w1", [DH, D]), ("L1_ffn_conv_w", [DH, 1, 3]),
        ("L1_ffn_conv_b", [DH]), ("L1_ffn_w2", [D, DH]),
    ]:
        w[name] = nc.dram_tensor(name, shape, F32, kind="ExternalInput").ap()
    w["L1_attn_sample_idx"] = nc.dram_tensor(
        "L1_attn_sample_idx", [S], I32, kind="ExternalInput").ap()
    out = nc.dram_tensor("out", [S, D], F32, kind="ExternalOutput").ap()

    scr = {}
    for nm in ["WQ0T", "WK0T", "WV0T", "WO0T", "WQ1T", "WK1T", "WV1T",
               "WO1T"]:
        scr[nm] = nc.dram_tensor(nm, [D, D], BF16, kind="Internal").ap()
    scr["W1QT"] = nc.dram_tensor("W1QT", [D, DH], BF16, kind="Internal").ap()
    scr["W2QT"] = nc.dram_tensor("W2QT", [DH, D], BF16, kind="Internal").ap()
    scr["XDQT"] = nc.dram_tensor("XDQT", [D, S + 2], BF16,
                                 kind="Internal").ap()
    scr["XNT0"] = nc.dram_tensor("XNT0", [D, S], BF16, kind="Internal").ap()
    scr["XNT1"] = nc.dram_tensor("XNT1", [D, S], BF16, kind="Internal").ap()
    scr["OUTT1"] = nc.dram_tensor("OUTT1", [D, S], BF16,
                                  kind="Internal").ap()

    for e in ENGINES:
        p.new_sem("b_" + e)
    for nm in ["te", "ve", "sc", "gp", "d_s", "d_g", "d_c", "d_v"]:
        p.new_sem(nm)

    res = st.enter_context(sbt("res", [128, NT * D], F32))
    junk = st.enter_context(sbt("junk", [128, 8], F32))
    junkB = st.enter_context(sbt("junkB", [128, 8], F32))
    junkG = st.enter_context(sbt("junkG", [128, 8], F32))
    junkW = st.enter_context(sbt("junkW", [128, 512], F32))
    p.settle_fn = {
        "vector": lambda eng: eng.memset(junk[:, 0:8], 0.0),
        "scalar": lambda eng: eng.copy(junkB[:, 0:8], junk[:, 0:8]),
        "gpsimd": lambda eng: eng.iota(
            junkG[:, 0:1], [[0, 1]], base=0, channel_multiplier=1,
            allow_small_or_imprecise_dtypes=True),
    }

    def VSP():
        p.emit("vector", lambda e: e.memset(junk[:, 0:8], 0.0))
    eye = st.enter_context(sbt("eye", [128, 128], BF16))
    eyef = st.enter_context(sbt("eyef", [128, 128], F32))
    onesf = st.enter_context(sbt("onesf", [1, 128], F32))
    onesbf = st.enter_context(sbt("onesbf", [128, 512], BF16))
    colf = st.enter_context(sbt("colf", [128, 64], F32))
    rowf = st.enter_context(sbt("rowf", [1, 512], F32))
    rowb = st.enter_context(sbt("rowb", [1, 512], BF16))
    tmpA = st.enter_context(sbt("tmpA", [128, 512], F32))
    tmpB = st.enter_context(sbt("tmpB", [128, 512], F32))
    tmpC = st.enter_context(sbt("tmpC", [128, 512], F32))
    sqb = st.enter_context(sbt("sqb", [128, 512], BF16))
    iotaT16 = st.enter_context(sbt("iotaT16", [128, NT], F32))
    cst = st.enter_context(sbt("cst", [128, 2], F32))
    psb = [st.enter_context(nc.psum_tensor(f"pb{i}", [128, 512], F32))
           for i in range(6)]
    banks = Banks(p, psb)
    psbT = [st.enter_context(nc.psum_tensor(f"pt{i}", [128, 1024], BF16))
            for i in range(2)]
    tp_state = {"rr": 0, "tok": [None, None]}

    def transpose_batch(srcs, dst):
        b = tp_state["rr"]
        tp_state["rr"] = 1 - b
        tok = tp_state["tok"][b]
        if tok is not None:
            p.wait("tensor", tok[0], tok[1])
        W("tensor", "ve", p.cnt["ve"])
        W("tensor", "sc", p.cnt["sc"])
        tv = None
        n = len(srcs)
        for i, s in enumerate(srcs):
            tv = V("tensor", lambda e, s=s, i=i, b=b: nc.tensor.transpose(
                psbT[b][:, ts(i, 128)], s, eye[:, :]),
                inc="te" if i == n - 1 else None)
        W("scalar", "te", tv)
        sv = V("scalar", lambda e, b=b, n=n, dst=dst: e.copy(
            dst, psbT[b][:, 0:n * 128]), inc="sc")
        tp_state["tok"][b] = ("sc", sv)
        return sv

    def bcast_row(dst, src_row, n, npart=128):
        last = 0
        W("tensor", "ve", p.cnt["ve"])
        W("tensor", "d_s", p.cnt["d_s"])
        for c0 in range(0, n, 512):
            cw = min(512, n - c0)
            bi = banks.acquire()
            tv = V("tensor", lambda e, bi=bi, c0=c0, cw=cw, npart=npart:
                   nc.tensor.matmul(psb[bi][0:npart, 0:cw],
                                    onesf[0:1, 0:npart],
                                    src_row[0:1, c0:c0 + cw],
                                    start=True, stop=True), inc="te")
            W("vector", "te", tv)
            last = V("vector", lambda e, bi=bi, c0=c0, cw=cw, npart=npart:
                     e.tensor_copy(dst[0:npart, c0:c0 + cw],
                                   psb[bi][0:npart, 0:cw]), inc="ve")
            banks.release(bi, "ve", last)
        return last

    def col_mod(dst, src_col, m, tmp1, tmp2):
        V("vector", lambda e: e.tensor_single_scalar(
            tmp1, src_col, 1.0 / m, ALU.mult), inc="ve")
        VSP()
        for bit in (256.0, 128.0, 64.0, 32.0, 16.0, 8.0, 4.0, 2.0, 1.0):
            V("vector", lambda e, bit=bit: e.tensor_single_scalar(
                tmp2, tmp1, bit, ALU.is_ge), inc="ve")
            VSP()
            V("vector", lambda e, bit=bit: e.scalar_tensor_tensor(
                tmp1, tmp2, -bit, tmp1, ALU.mult, ALU.add), inc="ve")
            VSP()
        return V("vector", lambda e: e.tensor_single_scalar(
            dst, tmp1, float(m), ALU.mult), inc="ve")

    def emit_round(dst, src_ap, scale_col, post_add, post_mul_col, tmpF,
                   tmpR, tmpC, n):
        V("vector", lambda e: e.tensor_scalar(
            tmpF, src_ap, scale_col, 128.5, ALU.mult, ALU.add), inc="ve")
        VSP()
        V("vector", lambda e: e.tensor_copy(tmpR, tmpF), inc="ve")
        VSP()
        for bit in (128.0, 64.0, 32.0, 16.0, 8.0, 4.0, 2.0, 1.0):
            V("vector", lambda e, bit=bit: e.tensor_single_scalar(
                tmpC, tmpR, bit, ALU.is_ge), inc="ve")
            VSP()
            V("vector", lambda e, bit=bit: e.scalar_tensor_tensor(
                tmpR, tmpC, -bit, tmpR, ALU.mult, ALU.add), inc="ve")
            VSP()
        V("vector", lambda e: e.tensor_sub(tmpF, tmpF, tmpR), inc="ve")
        VSP()
        return V("vector", lambda e: e.tensor_scalar(
            dst, tmpF, -128.0 + post_add, post_mul_col, ALU.add, ALU.mult),
            inc="ve")

    def dma(queue, dst, src, sem=None, slow_ok=False):
        sem = sem or ("d_g" if queue == "gpsimd" else
                      "d_c" if queue == "scalar" else
                      "d_v" if queue == "vector" else "d_s")
        return p.run(queue, lambda e, d=dst, s=src, so=slow_ok: e.dma_start(
            out=d, in_=s, allow_slow_non_contiguous=so), inc=sem, n=16)

    V("gpsimd", lambda e: e.iota(tmpA[:, 0:128], [[-1, 128]], base=0,
                                 channel_multiplier=1, allow_small_or_imprecise_dtypes=True), inc="gp")
    V("gpsimd", lambda e: e.iota(colf[:, 0:1], [[0, 1]], base=0,
                                 channel_multiplier=1, allow_small_or_imprecise_dtypes=True), inc="gp")
    gv = V("gpsimd", lambda e: e.iota(iotaT16[:, :], [[128, NT]], base=0,
                                      channel_multiplier=1, allow_small_or_imprecise_dtypes=True), inc="gp")
    W("vector", "gp", gv)
    V("vector", lambda e: e.tensor_single_scalar(
        eye[:, :], tmpA[:, 0:128], 0.0, ALU.is_equal), inc="ve")
    V("vector", lambda e: e.tensor_single_scalar(
        eyef[:, :], tmpA[:, 0:128], 0.0, ALU.is_equal), inc="ve")
    V("vector", lambda e: e.memset(onesbf[:, :], 1.0), inc="ve")
    V("vector", lambda e: e.memset(onesf[:, :], 1.0), inc="ve")
    V("vector", lambda e: e.memset(cst[:, 0:1], 1e-5), inc="ve")
    V("vector", lambda e: e.memset(cst[:, 1:2], 0.125), inc="ve")
    p.barrier()

    def prep_wT(src_ap, dst_ap):
        with sbt("pw_in", [128, D], BF16) as win, \
             sbt("pw_out", [128, D], BF16) as wout:
            last_store = [0]
            for ec in range(ND):
                W("gpsimd", "te", p.cnt["te"])
                dv = dma("gpsimd", win[:, :], src_ap[ts(ec, 128), :])
                W("tensor", "d_g", dv)
                W("scalar", "d_s", last_store[0])
                sv = transpose_batch([win[:, ts(dc, 128)]
                                      for dc in range(ND)], wout[:, :])
                W("sync", "sc", sv)
                for dc in range(ND):
                    last_store[0] = dma("sync",
                                        dst_ap[ts(dc, 128), ts(ec, 128)],
                                        wout[:, ts(dc, 128)])

    def prep_cast(src_ap, dst_ap):
        with sbt("pc_in", [128, D], BF16) as win:
            for rc in range(ND):
                W("gpsimd", "d_s", p.cnt["d_s"])
                dv = dma("gpsimd", win[:, :], src_ap[ts(rc, 128), :])
                W("sync", "d_g", dv)
                dma("sync", dst_ap[ts(rc, 128), :], win[:, :])

    for nm, srcnm in [("WQ0T", "L0_attn_W_Q"), ("WK0T", "L0_attn_W_K"),
                      ("WV0T", "L0_attn_W_V"), ("WO0T", "L0_attn_W_O")]:
        prep_wT(w[srcnm], scr[nm])
        p.barrier()
    for nm, srcnm in [("WQ1T", "L1_attn_Wq"), ("WK1T", "L1_attn_Wk"),
                      ("WV1T", "L1_attn_Wv"), ("WO1T", "L1_attn_Wo")]:
        prep_cast(w[srcnm], scr[nm])
        p.barrier()

    def prep_wq(src_ap, dst_ap, n_r, n_c, slot):
        nrc, ncc = n_r // 128, n_c // 128
        with sbt("q_in", [128, n_c], F32) as qin, \
             sbt("q_f", [128, n_c], F32) as qf, \
             sbt("q_t", [128, n_c], BF16) as qt, \
             sbt("q_acc", [128, 2], F32) as qacc, \
             sbt("q_ones", [128, 1], F32) as qones, \
             sbt("q_ev", [128, 1024], BF16) as qev:
            V("vector", lambda e: e.memset(qacc[:, 0:1], 0.0), inc="ve")
            V("vector", lambda e: e.memset(qones[:, :], 1.0), inc="ve")
            vv = p.cnt["ve"]
            for rc in range(nrc):
                W("sync", "ve", p.cnt["ve"])
                dv = dma("sync", qin[:, :], src_ap[ts(rc, 128), :])
                W("vector", "d_s", dv)
                V("vector", lambda e: e.tensor_reduce(
                    qacc[:, 1:2], qin[:, :], AX.X, ALU.add,
                    apply_absolute_value=True), inc="ve")
                VSP()
                vv = V("vector", lambda e: e.tensor_add(
                    qacc[:, 0:1], qacc[:, 0:1], qacc[:, 1:2]), inc="ve")
            vv = V("vector", lambda e: e.memset(junk[:, 0:8], 0.0), inc="ve")
            vv = V("vector", lambda e: e.memset(junk[:, 0:8], 0.0), inc="ve")
            W("tensor", "ve", vv)
            bi = banks.acquire()
            tv = V("tensor", lambda e, bi=bi: nc.tensor.matmul(
                psb[bi][0:1, 0:1], qones[:, :], qacc[:, 0:1],
                start=True, stop=True), inc="te")
            W("vector", "te", tv)
            V("vector", lambda e, bi=bi: e.tensor_single_scalar(
                colf[0:1, 60:61], psb[bi][0:1, 0:1], 1.0 / (n_r * n_c),
                ALU.mult), inc="ve")
            VSP()
            VSP()
            V("vector", lambda e: e.tensor_single_scalar(
                colf[0:1, 60:61], colf[0:1, 60:61], 1e-5, ALU.max), inc="ve")
            VSP()
            VSP()
            vv = V("vector", lambda e: e.reciprocal(
                colf[0:1, 61:62], colf[0:1, 60:61]), inc="ve")
            vv = V("vector", lambda e: e.memset(junk[:, 0:8], 0.0), inc="ve")
            vv = V("vector", lambda e: e.memset(junk[:, 0:8], 0.0), inc="ve")
            banks.release(bi, "ve", vv)
            bcast_row(colf[:, slot:slot + 1], colf[0:1, 60:61], 1)
            bcast_row(colf[:, slot + 1:slot + 2], colf[0:1, 61:62], 1)
            last_store = [0]
            for rc in range(nrc):
                W("sync", "ve", p.cnt["ve"])
                dv = dma("sync", qin[:, :], src_ap[ts(rc, 128), :])
                W("vector", "d_s", dv)
                W("vector", "te", p.cnt["te"])
                V("vector", lambda e: e.tensor_scalar(
                    qf[:, :], qin[:, :], colf[:, slot + 1:slot + 2], None,
                    ALU.mult), inc="ve")
                VSP()
                V("vector", lambda e: e.tensor_single_scalar(
                    qin[:, :], qf[:, :], 0.5, ALU.is_ge), inc="ve")
                V("vector", lambda e: e.tensor_single_scalar(
                    qf[:, :], qf[:, :], -0.5, ALU.is_le), inc="ve")
                VSP()
                vv = V("vector", lambda e: e.tensor_tensor(
                    qt[:, :], qin[:, :], qf[:, :], ALU.subtract), inc="ve")
                W("tensor", "ve", vv)
                for cb in range(ncc // 8):
                    W("scalar", "d_s", last_store[0])
                    sv = transpose_batch(
                        [qt[:, ts(cb * 8 + i, 128)] for i in range(8)],
                        qev[:, 0:1024])
                    W("sync", "sc", sv)
                    for i in range(8):
                        last_store[0] = dma(
                            "sync", dst_ap[ts(cb * 8 + i, 128), ts(rc, 128)],
                            qev[:, ts(i, 128)])

    prep_wq(w["L1_ffn_w1"], scr["W1QT"], DH, D, 40)
    p.barrier()
    prep_wq(w["L1_ffn_w2"], scr["W2QT"], D, DH, 42)
    p.barrier()

    for tt in range(NT):
        dma("sync", res[:, ts(tt, D)], xin[ts(tt, 128), :])
    p.barrier()

    def layernorm(xnt_name, xq=False):
        with sbt("ln_xn", [128, D], F32) as xn_tok, \
             sbt("ln_f", [128, D], F32) as lf, \
             sbt("ln_bf", [128, D], BF16) as xn_bf, \
             sbt("ln_q", [128, D], BF16) as xq_bf, \
             sbt("ln_qa", [128, D], F32) as qtA, \
             sbt("ln_qb", [128, D], F32) as qtB, \
             sbt("ln_ev", [128, 8 * 128], BF16) as lev:
            last_store = [0]
            for tt in range(NT):
                rsl = res[:, ts(tt, D)]
                W("vector", "te", p.cnt["te"])
                W("vector", "sc", p.cnt["sc"])
                W("scalar", "ve", p.cnt["ve"])
                V("vector", lambda e, rsl=rsl: e.tensor_reduce(
                    colf[:, 1:2], rsl, AX.X, ALU.add), inc="ve")
                sv = V("scalar", lambda e, rsl=rsl: e.activation(
                    lf[:, :], rsl, AF.Square, accum_out=colf[:, 2:3]),
                    inc="sc")
                W("vector", "sc", sv)
                V("vector", lambda e: e.tensor_single_scalar(
                    colf[:, 1:2], colf[:, 1:2], 1.0 / D, ALU.mult), inc="ve")
                V("vector", lambda e: e.tensor_single_scalar(
                    colf[:, 2:3], colf[:, 2:3], 1.0 / D, ALU.mult), inc="ve")
                V("vector", lambda e: e.tensor_tensor(
                    colf[:, 3:4], colf[:, 1:2], colf[:, 1:2], ALU.mult),
                    inc="ve")
                VSP()
                vv = V("vector", lambda e: e.tensor_sub(
                    colf[:, 2:3], colf[:, 2:3], colf[:, 3:4]), inc="ve")
                W("scalar", "ve", vv)
                sv = V("scalar", lambda e: e.activation(
                    colf[:, 3:4], colf[:, 2:3], AF.Sqrt, bias=cst[:, 0:1]), inc="sc")
                W("vector", "sc", sv)
                V("vector", lambda e: e.reciprocal(
                    colf[:, 3:4], colf[:, 3:4]), inc="ve")
                VSP()
                vv = V("vector", lambda e, rsl=rsl: e.tensor_scalar(
                    xn_tok[:, :], rsl, colf[:, 1:2], colf[:, 3:4],
                    ALU.subtract, ALU.mult), inc="ve")
                VSP()
                V("vector", lambda e: e.tensor_copy(
                    xn_bf[:, :], xn_tok[:, :]), inc="ve")
                if xq:
                    V("vector", lambda e: e.tensor_reduce(
                        colf[:, 4:5], xn_tok[:, :], AX.X, ALU.max,
                        apply_absolute_value=True), inc="ve")
                    VSP()
                    V("vector", lambda e: e.tensor_single_scalar(
                        colf[:, 4:5], colf[:, 4:5], 1e-5, ALU.max), inc="ve")
                    VSP()
                    V("vector", lambda e: e.reciprocal(
                        colf[:, 5:6], colf[:, 4:5]), inc="ve")
                    VSP()
                    V("vector", lambda e: e.tensor_single_scalar(
                        colf[:, 5:6], colf[:, 5:6], 127.0, ALU.mult),
                        inc="ve")
                    V("vector", lambda e: e.tensor_single_scalar(
                        colf[:, 6:7], colf[:, 4:5], 1.0 / 127.0, ALU.mult),
                        inc="ve")
                    vv = emit_round(xq_bf[:, :], xn_tok[:, :],
                                    colf[:, 5:6], 0.0, colf[:, 6:7],
                                    lf[:, :], qtA[:, 0:D], qtB[:, 0:D], D)
                W("tensor", "ve", vv)
                srcs = [(xn_bf, scr[xnt_name], 0)]
                if xq:
                    srcs.append((xq_bf, scr["XDQT"], 1))
                for sb, tgt, off in srcs:
                    W("scalar", "d_s", last_store[0])
                    sv = transpose_batch([sb[:, ts(dc, 128)]
                                          for dc in range(ND)], lev[:, :])
                    W("sync", "sc", sv)
                    for dc in range(ND):
                        cpos = tt * 128 + off
                        last_store[0] = dma(
                            "sync", tgt[ts(dc, 128), cpos:cpos + 128],
                            lev[:, ts(dc, 128)])
        if xq:
            with sbt("zc", [128, 1], BF16) as zc:
                vv = V("vector", lambda e: e.memset(zc[:, :], 0.0), inc="ve")
                W("sync", "ve", vv)
                for dc in range(ND):
                    dma("sync", scr["XDQT"][ts(dc, 128), 0:1], zc[:, :],
                        slow_ok=True)
                    dma("sync", scr["XDQT"][ts(dc, 128), S + 1:S + 2],
                        zc[:, :], slow_ok=True)
        p.barrier()

    def qk_bound(QT, KT, negc_slot, scale):
        p.barrier()
        with sbt("bn_ones", [128, 2], BF16) as bones, \
             sbt("bn_acc", [32, 2], F32) as bacc, \
             sbt("bn_sh", [32, 2], F32) as bsh:
            V("vector", lambda e: e.memset(bones[:, :], 0.0), inc="ve")
            V("vector", lambda e: e.memset(bones[0:64, 0:1], 1.0), inc="ve")
            V("vector", lambda e: e.memset(bones[64:128, 1:2], 1.0),
              inc="ve")
            vv = V("vector", lambda e: e.memset(bacc[:, :], 0.0), inc="ve")
            W("scalar", "ve", vv)
            for qi, srcT in ((0, QT), (1, KT)):
                for ec in range(ECH_G):
                    for tc in range(NTCH):
                        W("scalar", "te", p.cnt["te"])
                        sv = V("scalar", lambda e, ec=ec, tc=tc, srcT=srcT:
                               e.activation(
                                   sqb[:, :],
                                   srcT[:, ec * S + tc * 512:
                                        ec * S + tc * 512 + 512],
                                   AF.Square), inc="sc")
                        W("tensor", "sc", sv)
                        bi = banks.acquire()
                        tv = V("tensor", lambda e, bi=bi: nc.tensor.matmul(
                            psb[bi][0:2, :], bones[:, :], sqb[:, :],
                            start=True, stop=True), inc="te")
                        W("vector", "te", tv)
                        V("vector", lambda e, bi=bi: e.tensor_reduce(
                            colf[0:2, 12:13], psb[bi][0:2, :], AX.X,
                            ALU.max), inc="ve")
                        vv = V("vector", lambda e, qi=qi: e.tensor_max(
                            bacc[0:2, qi:qi + 1], bacc[0:2, qi:qi + 1],
                            colf[0:2, 12:13]), inc="ve")
                        banks.release(bi, "ve", vv)
            V("vector", lambda e: e.stream_shuffle(
                bsh[:, :], bacc[:, :], [i ^ 1 for i in range(32)]), inc="ve")
            vv = V("vector", lambda e: e.tensor_max(
                bacc[:, :], bacc[:, :], bsh[:, :]), inc="ve")
            W("scalar", "ve", vv)
            sv = V("scalar", lambda e: e.activation(
                bacc[0:1, :], bacc[0:1, :], AF.Sqrt), inc="sc")
            W("vector", "sc", sv)
            V("vector", lambda e: e.tensor_tensor(
                colf[0:1, 14:15], bacc[0:1, 0:1], bacc[0:1, 1:2], ALU.mult),
                inc="ve")
            vv = V("vector", lambda e, scale=scale: e.tensor_single_scalar(
                colf[0:1, 14:15], colf[0:1, 14:15], -scale, ALU.mult),
                inc="ve")
            lastb = bcast_row(colf[:, negc_slot:negc_slot + 1],
                              colf[0:1, 14:15], 1)
            W("scalar", "ve", lastb)

    def proj_feature(dstT, wT_name, xnt_scr, group, bias_src=None,
                     rope=None):
        p.barrier()
        with sbt("pf_l", [128, 2 * ND * 128], BF16) as lr, \
             sbt("pf_x", [128, ND * 512], BF16) as xr, \
             sbt("pf_b", [1, D], BF16) as brr:
            if bias_src is not None:
                dvb = dma("gpsimd", brr[:, :], bias_src[None, :])
                W("tensor", "d_g", dvb)
            lr_tok = {}
            seq = [(tc, ec) for tc in range(NTCH) for ec in range(ECH_G)]

            def load_l(ec, buf):
                dv = 0
                eg = group * ECH_G + ec
                for dc in range(ND):
                    dv = dma("gpsimd",
                             lr[:, buf * ND * 128 + dc * 128:
                                buf * ND * 128 + dc * 128 + 128],
                             scr[wT_name][ts(dc, 128), ts(eg, 128)])
                return dv

            lr_tok[0] = load_l(0, 0)
            xr_done = 0
            for si, (tc, ec) in enumerate(seq):
                buf = si % 2
                if ec == 0:
                    W("sync", "te", xr_done)
                    dvx = 0
                    for dc in range(ND):
                        dvx = dma("sync", xr[:, ts(dc, 512)],
                                  xnt_scr[ts(dc, 128), ts(tc, 512)])
                if si + 1 < len(seq):
                    W("gpsimd", "te", p.cnt["te"])
                    lr_tok[1 - buf] = load_l(seq[si + 1][1], 1 - buf)
                W("tensor", "d_g", lr_tok[buf])
                W("tensor", "d_s", dvx)
                bi = banks.acquire()
                tv = None
                for dc in range(ND):
                    last = dc == ND - 1
                    lstop = last and bias_src is None
                    tv = V("tensor",
                           lambda e, dc=dc, bi=bi, buf=buf, f=(dc == 0),
                           l=lstop: nc.tensor.matmul(
                               psb[bi][:, :],
                               lr[:, buf * ND * 128 + dc * 128:
                                  buf * ND * 128 + dc * 128 + 128],
                               xr[:, ts(dc, 512)], start=f, stop=l),
                           inc="te" if last else None)
                if bias_src is not None:
                    eg = group * ECH_G + ec
                    tv = V("tensor", lambda e, bi=bi, eg=eg:
                           nc.tensor.matmul(
                               psb[bi][:, :],
                               brr[0:1, eg * 128:eg * 128 + 128],
                               onesbf[0:1, :], start=False, stop=True),
                           inc="te")
                xr_done = tv
                dst_sl = dstT[:, ec * S + tc * 512:ec * S + tc * 512 + 512]
                if rope is None:
                    W("scalar", "te", tv)
                    sv = V("scalar", lambda e, bi=bi, dst_sl=dst_sl: e.copy(
                        dst_sl, psb[bi][:, :]), inc="sc")
                    banks.release(bi, "sc", sv)
                else:
                    ropeC, ropeS = rope
                    W("scalar", "te", tv)
                    W("scalar", "ve", p.cnt["ve"])
                    sv = V("scalar", lambda e, bi=bi: e.copy(
                        tmpA[:, :], psb[bi][:, :]), inc="sc")
                    W("vector", "sc", sv)
                    V("vector", lambda e: e.stream_shuffle(
                        tmpB[:, :], tmpA[:, :], [i ^ 1 for i in range(32)]),
                        inc="ve")
                    V("vector", lambda e, tc=tc: e.tensor_tensor(
                        tmpC[:, :], tmpA[:, :], ropeC[:, ts(tc, 512)],
                        ALU.mult), inc="ve")
                    V("vector", lambda e, tc=tc: e.tensor_tensor(
                        tmpB[:, :], tmpB[:, :], ropeS[:, ts(tc, 512)],
                        ALU.mult), inc="ve")
                    vv = V("vector", lambda e, dst_sl=dst_sl: e.tensor_add(
                        dst_sl, tmpC[:, :], tmpB[:, :]), inc="ve")
                    banks.release(bi, "ve", vv)

    def proj_token(vaug, wT_name, xnt_scr, group, bias_src=None):
        p.barrier()
        vv = V("vector", lambda e: e.memset(
            vaug[:, :].rearrange("p (t h c) -> p t h c", h=GRP, c=65)
            [:, :, :, 64:65], 1.0), inc="ve")
        W("scalar", "ve", vv)
        with sbt("pt_w", [128, ND * 512], BF16) as vr, \
             sbt("pt_x", [128, 2 * ND * 128], BF16) as xr, \
             sbt("pt_b", [1, D], BF16) as brr:
            if bias_src is not None:
                dvb = dma("gpsimd", brr[:, :], bias_src[None, :])
                W("tensor", "d_g", dvb)
            dvl = 0
            for dc in range(ND):
                dvl = dma("gpsimd", vr[:, ts(dc, 512)],
                          scr[wT_name][ts(dc, 128), ts(group, 512)])
            W("tensor", "d_g", dvl)
            xr_tok = [0, 0]
            for tt in range(NT):
                xb = tt % 2
                W("sync", "te", xr_tok[xb])
                dvx = 0
                for dc in range(ND):
                    dvx = dma("sync", xr[:, xb * ND * 128 + dc * 128:
                                         xb * ND * 128 + dc * 128 + 128],
                              xnt_scr[ts(dc, 128), tt * 128:tt * 128 + 128])
                W("tensor", "d_s", dvx)
                bi = banks.acquire()
                tv = None
                for dc in range(ND):
                    last = dc == ND - 1
                    lstop = last and bias_src is None
                    tv = V("tensor", lambda e, dc=dc, bi=bi, xb=xb,
                           f=(dc == 0), l=lstop: nc.tensor.matmul(
                               psb[bi][:, :],
                               xr[:, xb * ND * 128 + dc * 128:
                                  xb * ND * 128 + dc * 128 + 128],
                               vr[:, ts(dc, 512)], start=f, stop=l),
                           inc="te" if last else None)
                if bias_src is not None:
                    tv = V("tensor", lambda e, bi=bi: nc.tensor.matmul(
                        psb[bi][:, :], onesbf[0:1, 0:128],
                        brr[0:1, group * 512:group * 512 + 512],
                        start=False, stop=True), inc="te")
                xr_tok[xb] = tv
                W("scalar", "te", tv)
                dst = vaug[:, tt * GRP * 65:(tt + 1) * GRP * 65] \
                    .rearrange("p (h c) -> p h c", c=65)[:, :, 0:64]
                src = psb[bi][:, :].rearrange("p (h c) -> p h c", c=64)
                sv = V("scalar", lambda e, dst=dst, src=src: e.copy(
                    dst, src), inc="sc")
                banks.release(bi, "sc", sv)

    def l0_attention():
        with sbt("ctxT", [128, ND * S], BF16) as ctxT:
            def l0_group(g, QT, KT, Vaug):
                with sbt("ropeC", [128, S], F32) as ropeC, \
                     sbt("ropeS", [128, S], F32) as ropeS:
                    col_mod(colf[:, 7:8], colf[:, 0:1], 64.0,
                            colf[:, 30:31], colf[:, 31:32])
                    V("vector", lambda e: e.tensor_single_scalar(
                        colf[:, 8:9], colf[:, 7:8], 0.5, ALU.mult),
                        inc="ve")
                    col_mod(colf[:, 9:10], colf[:, 8:9], 1.0,
                            colf[:, 30:31], colf[:, 31:32])
                    V("vector", lambda e: e.tensor_sub(
                        colf[:, 8:9], colf[:, 8:9], colf[:, 9:10]),
                        inc="ve")
                    vv = V("vector", lambda e: e.tensor_single_scalar(
                        colf[:, 9:10], colf[:, 8:9],
                        -2.0 / 64.0 * math.log(10000.0), ALU.mult),
                        inc="ve")
                    W("scalar", "ve", vv)
                    sv = V("scalar", lambda e: e.activation(
                        colf[:, 10:11], colf[:, 9:10], AF.Exp),
                        inc="sc")
                    gv = V("gpsimd", lambda e: e.iota(
                        ropeS[:, :], [[1, S]], base=0,
                        channel_multiplier=0, allow_small_or_imprecise_dtypes=True), inc="gp")
                    W("vector", "sc", sv)
                    W("vector", "gp", gv)
                    col_mod(colf[:, 11:12], colf[:, 0:1], 2.0,
                            colf[:, 30:31], colf[:, 31:32])
                    V("vector", lambda e: e.tensor_scalar(
                        colf[:, 11:12], colf[:, 11:12], 2.0, -1.0,
                        ALU.mult, ALU.add), inc="ve")
                    for c in range(NTCH):
                        V("vector", lambda e, c=c: e.tensor_scalar(
                            tmpA[:, :], ropeS[:, ts(c, 512)],
                            colf[:, 10:11], None, ALU.mult), inc="ve")
                        for dst, phase, signed in (
                                (ropeC, math.pi / 2.0, False),
                                (ropeS, 0.0, True)):
                            V("vector", lambda e, ph=phase:
                              e.tensor_scalar(
                                  tmpB[:, :], tmpA[:, :], ph,
                                  1.0 / TWO_PI, ALU.add, ALU.mult),
                              inc="ve")
                            for bit in (256.0, 128.0, 64.0, 32.0, 16.0,
                                        8.0, 4.0, 2.0, 1.0):
                                V("vector", lambda e, bit=bit:
                                  e.tensor_single_scalar(
                                      tmpC[:, :], tmpB[:, :], bit,
                                      ALU.is_ge), inc="ve")
                                V("vector", lambda e, bit=bit:
                                  e.scalar_tensor_tensor(
                                      tmpB[:, :], tmpC[:, :], -bit,
                                      tmpB[:, :], ALU.mult, ALU.add),
                                  inc="ve")
                            V("vector", lambda e:
                              e.tensor_single_scalar(
                                  tmpB[:, :], tmpB[:, :], TWO_PI,
                                  ALU.mult), inc="ve")
                            V("vector", lambda e: e.tensor_single_scalar(
                                tmpC[:, :], tmpB[:, :], math.pi,
                                ALU.is_ge), inc="ve")
                            V("vector", lambda e:
                              e.scalar_tensor_tensor(
                                  tmpB[:, :], tmpC[:, :], -TWO_PI,
                                  tmpB[:, :], ALU.mult, ALU.add),
                              inc="ve")
                            V("vector", lambda e: e.tensor_single_scalar(
                                tmpC[:, :], tmpB[:, :], -math.pi,
                                ALU.is_lt), inc="ve")
                            vv = V("vector", lambda e:
                                   e.scalar_tensor_tensor(
                                       tmpB[:, :], tmpC[:, :], TWO_PI,
                                       tmpB[:, :], ALU.mult, ALU.add),
                                   inc="ve")
                            W("scalar", "ve", vv)
                            sv = V("scalar", lambda e, dst=dst, c=c:
                                   e.activation(dst[:, ts(c, 512)],
                                                tmpB[:, :], AF.Sin),
                                   inc="sc")
                            W("vector", "sc", sv)
                            if signed:
                                V("vector", lambda e, dst=dst, c=c:
                                  e.tensor_scalar(
                                      dst[:, ts(c, 512)],
                                      dst[:, ts(c, 512)],
                                      colf[:, 11:12], None, ALU.mult),
                                  inc="ve")
                    p.barrier()
                    proj_feature(QT, "WQ0T", scr["XNT0"], g,
                                 rope=(ropeC, ropeS))
                    proj_feature(KT, "WK0T", scr["XNT0"], g,
                                 rope=(ropeC, ropeS))
                proj_token(Vaug, "WV0T", scr["XNT0"], g)
                qk_bound(QT, KT, 15, 0.125)
                with sbt("PT0", [128, NT * 512], BF16) as PT, \
                     sbt("cn0", [64, 512], BF16) as cnb:
                    prev_k1 = [0, 0]
                    for qc in range(NTCH):
                        for hl in range(GRP):
                            W("scalar", "te", p.cnt["te"])
                            W("scalar", "ve", p.cnt["ve"])
                            h = g * GRP + hl
                            pbase = (hl % 2) * 64
                            chunk = hl // 2
                            tp = (64, 0) if pbase else None
                            pt_done = 0
                            for kt in range(NT):
                                bi = banks.acquire()
                                tv = V("tensor",
                                       lambda e, kt=kt, bi=bi, qc=qc,
                                       pbase=pbase, chunk=chunk, tp=tp:
                                       nc.tensor.matmul(
                                           psb[bi][:, :],
                                           KT[pbase:pbase + 64,
                                              chunk * S + kt * 128:
                                              chunk * S + kt * 128
                                              + 128],
                                           QT[pbase:pbase + 64,
                                              chunk * S + qc * 512:
                                              chunk * S + qc * 512
                                              + 512],
                                           start=True, stop=True,
                                           tile_position=tp), inc="te")
                                W("scalar", "te", tv)
                                sv = V("scalar", lambda e, bi=bi, kt=kt:
                                       e.activation(
                                           PT[:, ts(kt, 512)],
                                           psb[bi][:, :], AF.Exp,
                                           bias=colf[:, 15:16],
                                           scale=cst[:, 1:2]), inc="sc")
                                banks.release(bi, "sc", sv)
                                pt_done = sv
                            W("tensor", "sc", pt_done)
                            bi = banks.acquire()
                            tv = None
                            for kt in range(NT):
                                last = kt == NT - 1
                                tv = V("tensor",
                                       lambda e, kt=kt, bi=bi, hl=hl,
                                       f=(kt == 0), l=last:
                                       nc.tensor.matmul(
                                           psb[bi][0:65, :],
                                           Vaug[:, kt * GRP * 65
                                                + hl * 65:
                                                kt * GRP * 65 + hl * 65
                                                + 65],
                                           PT[:, ts(kt, 512)],
                                           start=f, stop=l),
                                       inc="te" if last else None)
                            W("scalar", "te", tv)
                            W("scalar", "ve", prev_k1[1])
                            sv = V("scalar", lambda e, bi=bi: e.copy(
                                rowf[0:1, :], psb[bi][64:65, :]),
                                inc="sc")
                            W("vector", "sc", sv)
                            W("vector", "te", prev_k1[0])
                            V("vector", lambda e: e.reciprocal(
                                rowf[0:1, :], rowf[0:1, :]), inc="ve")
                            vv = V("vector", lambda e: e.tensor_copy(
                                rowb[0:1, :], rowf[0:1, :]), inc="ve")
                            W("tensor", "ve", vv)
                            bj = banks.acquire()
                            tvb = V("tensor", lambda e, bj=bj:
                                    nc.tensor.matmul(
                                        psb[bj][0:64, :],
                                        onesbf[0:1, 0:64],
                                        rowb[0:1, :], start=True,
                                        stop=True), inc="te")
                            prev_k1[0] = tvb
                            W("scalar", "te", tvb)
                            sv = V("scalar", lambda e, bj=bj: e.copy(
                                cnb[:, :], psb[bj][0:64, :]), inc="sc")
                            banks.release(bj, "sc", sv)
                            prev_k1[1] = sv
                            W("vector", "sc", sv)
                            vv = V("vector",
                                   lambda e, bi=bi, h=h, qc=qc:
                                   e.tensor_tensor(
                                       ctxT[(h % 2) * 64:
                                            (h % 2) * 64 + 64,
                                            (h // 2) * S + qc * 512:
                                            (h // 2) * S + qc * 512
                                            + 512],
                                       psb[bi][0:64, :], cnb[:, :],
                                       ALU.mult), inc="ve")
                            banks.release(bi, "ve", vv)
            for g in range(NGRP):
                with sbt("QT0", [128, ECH_G * S], BF16) as QT, \
                     sbt("KT0", [128, ECH_G * S], BF16) as KT, \
                     sbt("Vaug0", [128, NT * GRP * 65],
                                    BF16) as Vaug:
                    l0_group(g, QT, KT, Vaug)
                p.barrier()
            with sbt("wo0", [128, ND * D], BF16) as wo:
                dv = 0
                for dc in range(ND):
                    dv = dma("gpsimd", wo[:, ts(dc, D)],
                             scr["WO0T"][ts(dc, 128), :])
                W("tensor", "d_g", dv)
                for tt in range(NT):
                    for ec2 in range(2):
                        bi = banks.acquire()
                        tv = None
                        for dc in range(ND):
                            last = dc == ND - 1
                            tv = V("tensor",
                                   lambda e, dc=dc, bi=bi, tt=tt, ec2=ec2,
                                   f=(dc == 0), l=last: nc.tensor.matmul(
                                       psb[bi][:, :],
                                       ctxT[:, dc * S + tt * 128:
                                            dc * S + tt * 128 + 128],
                                       wo[:, dc * D + ec2 * 512:
                                          dc * D + ec2 * 512 + 512],
                                       start=f, stop=l),
                                   inc="te" if last else None)
                        W("vector", "te", tv)
                        rsl = res[:, tt * D + ec2 * 512:
                                  tt * D + ec2 * 512 + 512]
                        vv = V("vector", lambda e, bi=bi, rsl=rsl:
                               e.tensor_add(rsl, rsl, psb[bi][:, :]),
                               inc="ve")
                        banks.release(bi, "ve", vv)
        p.barrier()

    def l0_moe():
        with sbt("xnT0", [128, ND * S], BF16) as xnT, \
             sbt("W8", [128, NT * NE], F32) as W8:
            for dc in range(ND):
                dma("sync", xnT[:, ts(dc, S)], scr["XNT0"][ts(dc, 128), :])
            p.barrier()
            with sbt("rw", [128, ND * 8], BF16) as rw, \
                 sbt("r8a", [128, 8], F32) as r8a, \
                 sbt("r8b", [128, 8], F32) as r8b, \
                 sbt("r8c", [128, 8], F32) as r8c, \
                 sbt("io8", [128, 8], F32) as io8, \
                 sbt("bg8", [128, 8], F32) as bg8:
                dv = 0
                for dc in range(ND):
                    dv = dma("gpsimd", rw[:, ts(dc, 8)],
                             w["L0_ffn_router_w"][ts(dc, 128), :])
                gv = V("gpsimd", lambda e: e.iota(
                    io8[:, :], [[1, 8]], base=0, channel_multiplier=0, allow_small_or_imprecise_dtypes=True),
                    inc="gp")
                W("vector", "gp", gv)
                V("vector", lambda e: e.tensor_single_scalar(
                    bg8[:, :], io8[:, :], -99.0, ALU.add), inc="ve")
                W("tensor", "d_g", dv)
                W("vector", "gp", gv)
                for tt in range(NT):
                    W("vector", "sc", p.cnt["sc"])
                    bi = banks.acquire()
                    tv = None
                    for dc in range(ND):
                        last = dc == ND - 1
                        tv = V("tensor", lambda e, dc=dc, bi=bi, tt=tt,
                               f=(dc == 0), l=last: nc.tensor.matmul(
                                   psb[bi][:, 0:8],
                                   xnT[:, dc * S + tt * 128:
                                       dc * S + tt * 128 + 128],
                                   rw[:, ts(dc, 8)], start=f, stop=l),
                               inc="te" if last else None)
                    W("vector", "te", tv)
                    W("scalar", "te", tv)
                    V("scalar", lambda e, bi=bi: e.copy(
                        junkB[:, 0:8], psb[bi][:, 0:8]), inc="sc")
                    V("vector", lambda e, bi=bi: e.tensor_copy(
                        r8a[:, :], psb[bi][:, 0:8]), inc="ve")
                    VSP(); VSP(); VSP(); VSP()
                    vv = V("vector", lambda e: e.tensor_reduce(
                        colf[:, 16:17], r8a[:, :], AX.X, ALU.max), inc="ve")
                    banks.release(bi, "ve", vv)
                    VSP(); VSP(); VSP()
                    V("vector", lambda e: e.tensor_scalar(
                        r8b[:, :], r8a[:, :], colf[:, 16:17], None,
                        ALU.is_equal), inc="ve")
                    VSP(); VSP()
                    V("vector", lambda e: e.tensor_tensor(
                        r8c[:, :], r8b[:, :], bg8[:, :], ALU.mult), inc="ve")
                    VSP(); VSP()
                    V("vector", lambda e: e.tensor_single_scalar(
                        r8c[:, :], r8c[:, :], 99.0, ALU.add), inc="ve")
                    VSP(); VSP()
                    V("vector", lambda e: e.tensor_reduce(
                        colf[:, 17:18], r8c[:, :], AX.X, ALU.min), inc="ve")
                    VSP(); VSP(); VSP()
                    V("vector", lambda e: e.tensor_scalar(
                        r8b[:, :], io8[:, :], colf[:, 17:18], None,
                        ALU.is_equal), inc="ve")
                    VSP(); VSP()
                    V("vector", lambda e: e.scalar_tensor_tensor(
                        r8c[:, :], r8b[:, :], -1e30, r8a[:, :],
                        ALU.mult, ALU.add), inc="ve")
                    VSP(); VSP()
                    V("vector", lambda e: e.tensor_reduce(
                        colf[:, 18:19], r8c[:, :], AX.X, ALU.max), inc="ve")
                    VSP(); VSP(); VSP()
                    V("vector", lambda e: e.tensor_scalar(
                        r8a[:, :], r8c[:, :], colf[:, 18:19], None,
                        ALU.is_equal), inc="ve")
                    VSP(); VSP()
                    V("vector", lambda e: e.tensor_tensor(
                        r8c[:, :], r8a[:, :], bg8[:, :], ALU.mult), inc="ve")
                    VSP(); VSP()
                    V("vector", lambda e: e.tensor_single_scalar(
                        r8c[:, :], r8c[:, :], 99.0, ALU.add), inc="ve")
                    VSP(); VSP()
                    V("vector", lambda e: e.tensor_reduce(
                        colf[:, 19:20], r8c[:, :], AX.X, ALU.min), inc="ve")
                    VSP(); VSP(); VSP()
                    V("vector", lambda e: e.tensor_scalar(
                        r8c[:, :], io8[:, :], colf[:, 19:20], None,
                        ALU.is_equal), inc="ve")
                    VSP(); VSP()
                    vv = V("vector", lambda e: e.tensor_sub(
                        colf[:, 20:21], colf[:, 16:17], colf[:, 18:19]),
                        inc="ve")
                    W("scalar", "ve", vv)
                    sv = V("scalar", lambda e: e.activation(
                        colf[:, 21:22], colf[:, 20:21], AF.Sigmoid),
                        inc="sc")
                    W("vector", "sc", sv)
                    V("vector", lambda e: e.tensor_scalar(
                        colf[:, 22:23], colf[:, 21:22], -1.0, 1.0,
                        ALU.mult, ALU.add), inc="ve")
                    V("vector", lambda e: e.tensor_scalar(
                        r8b[:, :], r8b[:, :], colf[:, 21:22], None,
                        ALU.mult), inc="ve")
                    V("vector", lambda e, tt=tt: e.scalar_tensor_tensor(
                        W8[:, ts(tt, 8)], r8c[:, :], colf[:, 22:23],
                        r8b[:, :], ALU.mult, ALU.add), inc="ve")
            p.barrier()
            NTP = MOE_TP // 512
            NTT = MOE_TP // 128
            with sbt("acts", [128, NFG * MOE_TP], BF16) as acts, \
                 sbt("gub", [128, 2 * ND * 256], BF16) as gub, \
                 sbt("dsl", [128, NFG * 512], BF16) as dsl, \
                 sbt("sil", [128, 512], F32) as sil:

                def load_gu(eidx, fg, buf):
                    dv = 0
                    base = buf * ND * 256
                    for mat, src in ((0, w["L0_ffn_gate_proj"]),
                                     (1, w["L0_ffn_up_proj"])):
                        for dc in range(ND):
                            dv = dma("gpsimd",
                                     gub[:, base + dc * 256 + mat * 128:
                                         base + dc * 256 + mat * 128 + 128],
                                     src[eidx, ts(dc, 128),
                                         fg * 128:fg * 128 + 128])
                    return dv

                gu_tok = {}
                seq = [(tp, e2, fg) for tp in range(N_MOE_TP)
                       for e2 in range(NE) for fg in range(NFG)]
                gu_tok[0] = load_gu(0, 0, 0)
                down_done = 0
                for si, (tp, eidx, fg) in enumerate(seq):
                    buf = si % 2
                    if si + 1 < len(seq):
                        ntp, ne2, nfg = seq[si + 1]
                        W("gpsimd", "te", p.cnt["te"])
                        gu_tok[1 - buf] = load_gu(ne2, nfg, 1 - buf)
                    W("tensor", "d_g", gu_tok[buf])
                    base = buf * ND * 256
                    for tc in range(NTP):
                        big = banks.acquire()
                        biu = banks.acquire()
                        for mat, bsel in ((0, big), (1, biu)):
                            tv = None
                            for dc in range(ND):
                                last = dc == ND - 1
                                tv = V("tensor",
                                       lambda e, dc=dc, bi=bsel, mat=mat,
                                       tc=tc, tp=tp, base=base, f=(dc == 0),
                                       l=last: nc.tensor.matmul(
                                           psb[bi][:, :],
                                           gub[:, base + dc * 256
                                               + mat * 128:
                                               base + dc * 256 + mat * 128
                                               + 128],
                                           xnT[:, dc * S + tp * MOE_TP
                                               + tc * 512:
                                               dc * S + tp * MOE_TP
                                               + tc * 512 + 512],
                                           start=f, stop=l),
                                       inc="te" if last else None)
                        W("scalar", "te", tv)
                        W("scalar", "ve", p.cnt["ve"])
                        sv = V("scalar", lambda e, bi=big: e.activation(
                            sil[:, :], psb[bi][:, :], AF.Silu), inc="sc")
                        banks.release(big, "sc", sv)
                        W("vector", "sc", sv)
                        W("vector", "te", down_done)
                        vv = V("vector", lambda e, bi=biu, fg=fg, tc=tc:
                               e.tensor_tensor(
                                   acts[:, fg * MOE_TP + tc * 512:
                                        fg * MOE_TP + tc * 512 + 512],
                                   sil[:, :], psb[bi][:, :], ALU.mult),
                               inc="ve")
                        banks.release(biu, "ve", vv)
                    if fg == NFG - 1:
                        W("tensor", "ve", p.cnt["ve"])
                        for ec2 in range(2):
                            W("gpsimd", "te", p.cnt["te"])
                            dtok = 0
                            for fgd in range(NFG):
                                dtok = dma(
                                    "gpsimd", dsl[:, ts(fgd, 512)],
                                    w["L0_ffn_down_proj"][eidx,
                                                          ts(fgd, 128),
                                                          ts(ec2, 512)])
                            W("tensor", "d_g", dtok)
                            for tlh in range(NTT // 4):
                                dbanks = [banks.acquire() for _ in range(4)]
                                tvs = [None] * 4
                                for fgd in range(NFG):
                                    for tl4 in range(4):
                                        tl = tlh * 4 + tl4
                                        last = fgd == NFG - 1
                                        tvs[tl4] = V(
                                            "tensor",
                                            lambda e, fgd=fgd, tl=tl,
                                            tl4=tl4, bi=dbanks[tl4],
                                            f=(fgd == 0), l=last:
                                            nc.tensor.matmul(
                                                psb[bi][:, :],
                                                acts[:, fgd * MOE_TP
                                                     + tl * 128:
                                                     fgd * MOE_TP
                                                     + tl * 128 + 128],
                                                dsl[:, ts(fgd, 512)],
                                                start=f, stop=l),
                                            inc="te" if last else None)
                                for tl4 in range(4):
                                    W("vector", "te", tvs[tl4])
                                    VSP()
                                    VSP()
                                    VSP()
                                    ttg = tp * NTT + tlh * 4 + tl4
                                    rsl = res[:, ttg * D + ec2 * 512:
                                              ttg * D + ec2 * 512 + 512]
                                    vv = V("vector",
                                           lambda e, bi=dbanks[tl4],
                                           rsl=rsl, ttg=ttg, eidx=eidx:
                                           e.scalar_tensor_tensor(
                                               rsl, psb[bi][:, :],
                                               W8[:, ttg * 8 + eidx:
                                                  ttg * 8 + eidx + 1],
                                               rsl, ALU.mult, ALU.add),
                                           inc="ve")
                                    banks.release(dbanks[tl4], "ve", vv)
                        down_done = p.cnt["te"]
        p.barrier()

    def l1_attention():
        with sbt("iotaQ", [128, 128], F32) as iotaQ, \
             sbt("sidb", [128, S], FP16) as sidb, \
             sbt("vmb", [1, 1024], BF16) as vmb:
            gv = V("gpsimd", lambda e: e.iota(
                iotaQ[:, :], [[1, 128]], base=0, channel_multiplier=128,
                allow_small_or_imprecise_dtypes=True), inc="gp")
            W("vector", "gp", gv)
            col_mod(colf[:, 24:25], colf[:, 0:1], 16.0,
                    colf[:, 30:31], colf[:, 31:32])
            V("vector", lambda e: e.tensor_sub(
                colf[:, 25:26], colf[:, 0:1], colf[:, 24:25]), inc="ve")
            V("vector", lambda e: e.tensor_single_scalar(
                colf[:, 25:26], colf[:, 25:26], 128.0, ALU.mult), inc="ve")
            V("vector", lambda e: e.tensor_scalar(
                iotaQ[:, :], iotaQ[:, :], colf[:, 25:26], None,
                ALU.subtract), inc="ve")
            with sbt("sid_i", [1, S], I32) as sid_i, \
                 sbt("sid_f", [1, S], F32) as sid_f:
                dv = dma("sync", sid_i[:, :],
                         w["L1_attn_sample_idx"][None, :])
                W("vector", "d_s", dv)
                V("vector", lambda e: e.tensor_copy(
                    sid_f[:, :], sid_i[:, :]), inc="ve")
                bcast_row(sidb[:, :], sid_f[0:1, :], S)
            p.barrier()

            def l1_group(g, QT, KT, Vaug, Qtok, Mcols, Msum):
                proj_feature(QT, "WQ1T", scr["XNT1"], g,
                             bias_src=w["L1_attn_bq"])
                proj_feature(KT, "WK1T", scr["XNT1"], g,
                             bias_src=w["L1_attn_bk"])
                proj_token(Vaug, "WV1T", scr["XNT1"], g,
                           bias_src=w["L1_attn_bv"])
                with sbt("Ktok", [128, NT * 512],
                                    BF16) as Ktok, \
                     sbt("KS", [128, ECH_G * S], BF16) as KS, \
                     sbt("Sjc", [128, NT * 256], BF16) as Sjc:
                    for srcT, dstTok in ((QT, Qtok), (KT, Ktok)):
                        for tt in range(NT):
                            dst = dstTok[:, tt * 512:tt * 512 + 512] \
                                .rearrange("p (ec c) -> p ec c", c=128)
                            sv = transpose_batch(
                                [srcT[:, ec * S + tt * 128:
                                      ec * S + tt * 128 + 128]
                                 for ec in range(ECH_G)],
                                dstTok[:, tt * 512:tt * 512 + 512])
                    for jc in range(S // 256):
                        W("vector", "te", p.cnt["te"])
                        vv = None
                        for tt in range(NT):
                            vv = V("vector", lambda e, tt=tt, jc=jc:
                                   e.tensor_scalar(
                                       Sjc[:, ts(tt, 256)],
                                       sidb[:, ts(jc, 256)],
                                       iotaT16[:, tt:tt + 1], None,
                                       ALU.is_equal), inc="ve")
                        W("tensor", "ve", vv)
                        for pc in range(ECH_G):
                            bi = banks.acquire()
                            tv = None
                            for tt in range(NT):
                                last = tt == NT - 1
                                tv = V("tensor",
                                       lambda e, tt=tt, bi=bi, pc=pc,
                                       f=(tt == 0), l=last:
                                       nc.tensor.matmul(
                                           psb[bi][:, 0:256],
                                           Ktok[:, tt * 512 + pc * 128:
                                                tt * 512 + pc * 128
                                                + 128],
                                           Sjc[:, ts(tt, 256)],
                                           start=f, stop=l),
                                       inc="te" if last else None)
                            W("scalar", "te", tv)
                            sv = V("scalar",
                                   lambda e, bi=bi, pc=pc, jc=jc:
                                   e.copy(KS[:, pc * S + jc * 256:
                                             pc * S + jc * 256 + 256],
                                          psb[bi][:, 0:256]), inc="sc")
                            banks.release(bi, "sc", sv)
                    with sbt("ksc", [128, ECH_G],
                                        BF16) as ksc, \
                         sbt("kscf", [128, ECH_G],
                                        F32) as kscf:
                        W("vector", "sc", p.cnt["sc"])
                        for pc in range(ECH_G):
                            V("vector", lambda e, pc=pc: e.tensor_reduce(
                                kscf[:, pc:pc + 1], KS[:, ts(pc, S)],
                                AX.X, ALU.add), inc="ve")
                        vv = V("vector", lambda e: e.tensor_copy(
                            ksc[:, :], kscf[:, :]), inc="ve")
                        W("tensor", "ve", vv)
                        for hl in range(GRP):
                            pbase = (hl % 2) * 64
                            chunk = hl // 2
                            tp2 = (64, 0) if pbase else None
                            bi = banks.acquire()
                            tv = None
                            for qt in range(NT):
                                tv = V("tensor",
                                       lambda e, qt=qt, bi=bi,
                                       pbase=pbase, chunk=chunk,
                                       tp2=tp2: nc.tensor.matmul(
                                           psb[bi][:, qt:qt + 1],
                                           QT[pbase:pbase + 64,
                                              chunk * S + qt * 128:
                                              chunk * S + qt * 128
                                              + 128],
                                           ksc[pbase:pbase + 64,
                                               chunk:chunk + 1],
                                           start=True, stop=True,
                                           tile_position=tp2),
                                       inc="te")
                            W("vector", "te", tv)
                            vv = V("vector", lambda e, bi=bi, hl=hl:
                                   e.tensor_copy(Msum[:, ts(hl, NT)],
                                                 psb[bi][:, 0:NT]),
                                   inc="ve")
                            banks.release(bi, "ve", vv)
                    vv = V("vector", lambda e: e.memset(
                        Mcols[:, :], -1e30), inc="ve")
                    for hl in range(GRP):
                        pbase = (hl % 2) * 64
                        chunk = hl // 2
                        tp2 = (64, 0) if pbase else None
                        for qt in range(NT):
                            for jc in range(NTCH):
                                bi = banks.acquire()
                                tv = V("tensor",
                                       lambda e, bi=bi, qt=qt, jc=jc,
                                       pbase=pbase, chunk=chunk,
                                       tp2=tp2: nc.tensor.matmul(
                                           psb[bi][:, :],
                                           QT[pbase:pbase + 64,
                                              chunk * S + qt * 128:
                                              chunk * S + qt * 128
                                              + 128],
                                           KS[pbase:pbase + 64,
                                              chunk * S + jc * 512:
                                              chunk * S + jc * 512
                                              + 512],
                                           start=True, stop=True,
                                           tile_position=tp2),
                                       inc="te")
                                W("vector", "te", tv)
                                V("vector", lambda e, bi=bi:
                                  e.tensor_reduce(
                                      colf[:, 26:27], psb[bi][:, :],
                                      AX.X, ALU.max), inc="ve")
                                col = hl * NT + qt
                                vv = V("vector", lambda e, col=col:
                                       e.tensor_max(
                                           Mcols[:, col:col + 1],
                                           Mcols[:, col:col + 1],
                                           colf[:, 26:27]), inc="ve")
                                banks.release(bi, "ve", vv)
                vv = V("vector", lambda e: e.scalar_tensor_tensor(
                    Mcols[:, :], Msum[:, :], -1.0 / S, Mcols[:, :],
                    ALU.mult, ALU.add), inc="ve")
                with ExitStack() as exs:
                    en = exs.enter_context
                    M8 = en(sbt("M8", [128, 128], F32))
                    idxm = en(sbt("idxm", [128, U1], F32))
                    t32a = en(sbt("t32a", [128, 1], F32))
                    t32b = en(sbt("t32b", [128, 1], F32))
                    e128 = en(sbt("e128", [128, 128], F32))
                    bg128 = en(sbt("bg1", [128, 128], F32))
                    idxr = en(sbt("idxr", [1, GRP * U1], F32))
                    idxb = en(sbt("idxb", [128, GRP * U1],
                                             F32))
                    idxmT = en(sbt("idxmT", [U1, 128], F32))
                    OHT = en(sbt("OHT", [128, NT * GRP * U1],
                                            BF16))
                    Qr = en(sbt("Qr", [128, ECH_G * U1],
                                           BF16))
                    PT1 = en(sbt("PT1", [128, NT * GRP * U1],
                                            BF16))
                    ctxn = en(sbt("ctxn", [U1, GRP * 65],
                                             BF16))
                    iotaTt = en(sbt("iotaTt", [U1, S], FP16))
                    W("gpsimd", "te", p.cnt["te"])
                    W("gpsimd", "ve", p.cnt["ve"])
                    W("gpsimd", "sc", p.cnt["sc"])
                    gvi = V("gpsimd", lambda e: e.iota(
                        iotaTt[:, :], [[1, S]], base=0,
                        channel_multiplier=0,
                        allow_small_or_imprecise_dtypes=True), inc="gp")
                    W("vector", "gp", gvi)
                    W("tensor", "ve", vv)
                    bi_t = banks.acquire()
                    tvt = V("tensor", lambda e, bi=bi_t:
                            nc.tensor.transpose(
                                psb[bi][:, 0:128], Mcols[:, :],
                                eyef[:, :]), inc="te")
                    W("vector", "te", tvt)
                    vvt = V("vector", lambda e, bi=bi_t: e.tensor_copy(
                        M8[:, :], psb[bi][:, 0:128]), inc="ve")
                    banks.release(bi_t, "ve", vvt)
                    VSP(); VSP(); VSP(); VSP()
                    V("vector", lambda e: e.tensor_single_scalar(
                        bg128[:, :], iotaQ[:, :], -1e6, ALU.add),
                      inc="ve")
                    qk_bound(QT, KT, 27, 0.125)
                    for it in range(U1):
                        V("vector", lambda e: e.tensor_reduce(
                            t32a[:, :], M8[:, :], AX.X, ALU.max),
                            inc="ve")
                        VSP(); VSP()
                        for k in (8, 4, 2, 1):
                            V("vector", lambda e, k=k: e.stream_shuffle(
                                t32b[:, :], t32a[:, :],
                                [i ^ k for i in range(32)]), inc="ve")
                            VSP(); VSP()
                            V("vector", lambda e: e.tensor_max(
                                t32a[:, :], t32a[:, :], t32b[:, :]),
                                inc="ve")
                            VSP(); VSP()
                        V("vector", lambda e: e.tensor_scalar(
                            e128[:, :], M8[:, :], t32a[:, :], None,
                            ALU.is_equal), inc="ve")
                        VSP(); VSP()
                        V("vector", lambda e: e.tensor_tensor(
                            e128[:, :], e128[:, :], bg128[:, :],
                            ALU.mult), inc="ve")
                        VSP(); VSP()
                        V("vector", lambda e: e.tensor_single_scalar(
                            e128[:, :], e128[:, :], 1e6, ALU.add),
                            inc="ve")
                        VSP(); VSP()
                        V("vector", lambda e: e.tensor_reduce(
                            t32a[:, :], e128[:, :], AX.X, ALU.min),
                            inc="ve")
                        VSP(); VSP()
                        for k in (8, 4, 2, 1):
                            V("vector", lambda e, k=k: e.stream_shuffle(
                                t32b[:, :], t32a[:, :],
                                [i ^ k for i in range(32)]), inc="ve")
                            VSP(); VSP()
                            V("vector", lambda e: e.tensor_tensor(
                                t32a[:, :], t32a[:, :], t32b[:, :],
                                ALU.min), inc="ve")
                            VSP(); VSP()
                        V("vector", lambda e, it=it: e.tensor_copy(
                            idxm[:, it:it + 1], t32a[:, :]), inc="ve")
                        V("vector", lambda e: e.tensor_scalar(
                            e128[:, :], iotaQ[:, :], t32a[:, :], None,
                            ALU.is_equal), inc="ve")
                        VSP(); VSP()
                        V("vector", lambda e: e.scalar_tensor_tensor(
                            M8[:, :], e128[:, :], -1e30, M8[:, :],
                            ALU.mult, ALU.add), inc="ve")
                        VSP(); VSP(); VSP()
                    vv = p.cnt["ve"]
                    W("sync", "ve", vv)
                    for hl in range(GRP):
                        dma("sync", idxr[0:1, ts(hl, U1)],
                            idxm[hl * 16:hl * 16 + 1, :])
                    W("tensor", "ve", p.cnt["ve"])
                    bi_t = banks.acquire()
                    tvt = V("tensor", lambda e, bi=bi_t:
                            nc.tensor.transpose(
                                psb[bi][0:U1, 0:128], idxm[:, :],
                                eyef[:, :]), inc="te")
                    W("vector", "te", tvt)
                    vvt = V("vector", lambda e, bi=bi_t: e.tensor_copy(
                        idxmT[:, :], psb[bi][0:U1, 0:128]), inc="ve")
                    banks.release(bi_t, "ve", vvt)
                    W("vector", "d_s", p.cnt["d_s"])
                    bcast_row(idxb[:, :], idxr[0:1, :], GRP * U1)
                    W("vector", "ve", vvt)
                    oht_done = None
                    for tt in range(NT):
                        oht_done = V("vector", lambda e, tt=tt:
                                     e.tensor_scalar(
                                         OHT[:, ts(tt, GRP * U1)],
                                         idxb[:, :],
                                         iotaT16[:, tt:tt + 1], None,
                                         ALU.is_equal), inc="ve")
                    W("tensor", "ve", oht_done)
                    for hl in range(GRP):
                        bi = banks.acquire()
                        tv = None
                        for tt in range(NT):
                            last = tt == NT - 1
                            tv = V("tensor",
                                   lambda e, tt=tt, bi=bi, hl=hl,
                                   f=(tt == 0), l=last:
                                   nc.tensor.matmul(
                                       psb[bi][0:64, 0:U1],
                                       Qtok[:, tt * 512
                                            + (hl // 2) * 128
                                            + (hl % 2) * 64:
                                            tt * 512 + (hl // 2) * 128
                                            + (hl % 2) * 64 + 64],
                                       OHT[:, tt * GRP * U1 + hl * U1:
                                           tt * GRP * U1 + hl * U1
                                           + U1],
                                       start=f, stop=l),
                                   inc="te" if last else None)
                        W("scalar", "te", tv)
                        sv = V("scalar", lambda e, bi=bi, hl=hl: e.copy(
                            Qr[(hl % 2) * 64:(hl % 2) * 64 + 64,
                               (hl // 2) * U1:(hl // 2) * U1 + U1],
                            psb[bi][0:64, 0:U1]), inc="sc")
                        banks.release(bi, "sc", sv)
                    W("tensor", "sc", p.cnt["sc"])
                    for hl in range(GRP):
                        pbase = (hl % 2) * 64
                        chunk = hl // 2
                        tp2 = (64, 0) if pbase else None
                        for tt in range(NT):
                            bi = banks.acquire()
                            tv = V("tensor",
                                   lambda e, bi=bi, tt=tt, pbase=pbase,
                                   chunk=chunk, tp2=tp2:
                                   nc.tensor.matmul(
                                       psb[bi][:, 0:U1],
                                       KT[pbase:pbase + 64,
                                          chunk * S + tt * 128:
                                          chunk * S + tt * 128 + 128],
                                       Qr[pbase:pbase + 64,
                                          chunk * U1:chunk * U1 + U1],
                                       start=True, stop=True,
                                       tile_position=tp2), inc="te")
                            W("scalar", "te", tv)
                            sv = V("scalar",
                                   lambda e, bi=bi, tt=tt, hl=hl:
                                   e.activation(
                                       PT1[:, tt * GRP * U1 + hl * U1:
                                           tt * GRP * U1 + hl * U1
                                           + U1],
                                       psb[bi][:, 0:U1], AF.Exp,
                                       bias=colf[:, 27:28],
                                       scale=cst[:, 1:2]), inc="sc")
                            banks.release(bi, "sc", sv)
                    V("vector", lambda e: e.memset(
                        ctxn[:, :].rearrange("p (h c) -> p h c", c=65)
                        [:, :, 64:65], 1.0), inc="ve")
                    W("scalar", "ve", p.cnt["ve"])
                    W("tensor", "sc", p.cnt["sc"])
                    for hl in range(GRP):
                        bi = banks.acquire()
                        tv = None
                        for tt in range(NT):
                            last = tt == NT - 1
                            tv = V("tensor",
                                   lambda e, tt=tt, bi=bi, hl=hl,
                                   f=(tt == 0), l=last:
                                   nc.tensor.matmul(
                                       psb[bi][0:U1, 0:65],
                                       PT1[:, tt * GRP * U1 + hl * U1:
                                           tt * GRP * U1 + hl * U1
                                           + U1],
                                       Vaug[:, tt * GRP * 65 + hl * 65:
                                            tt * GRP * 65 + hl * 65
                                            + 65],
                                       start=f, stop=l),
                                   inc="te" if last else None)
                        W("vector", "te", tv)
                        vv = V("vector", lambda e, bi=bi: e.reciprocal(
                            colf[0:U1, 28:29], psb[bi][0:U1, 64:65]),
                            inc="ve")
                        W("scalar", "ve", vv)
                        sv = V("scalar", lambda e, bi=bi, hl=hl:
                               e.activation(
                                   ctxn[0:U1, hl * 65:hl * 65 + 64],
                                   psb[bi][0:U1, 0:64], AF.Copy,
                                   scale=colf[0:U1, 28:29]), inc="sc")
                        banks.release(bi, "sc", sv)
                    bi = banks.acquire()
                    tv = None
                    for hl in range(GRP):
                        for tt in range(NT):
                            last = tt == NT - 1
                            tv = V("tensor",
                                   lambda e, tt=tt, hl=hl, bi=bi,
                                   f=(tt == 0), l=last:
                                   nc.tensor.matmul(
                                       psb[bi][0:1,
                                               hl * 64:hl * 64 + 64],
                                       onesbf[:, 0:1],
                                       Vaug[:, tt * GRP * 65 + hl * 65:
                                            tt * GRP * 65 + hl * 65
                                            + 64],
                                       start=f, stop=l),
                                   inc="te" if last else None)
                    W("vector", "te", tv)
                    vv = V("vector", lambda e, bi=bi, g=g:
                           e.tensor_single_scalar(
                               vmb[0:1, ts(g, 512)], psb[bi][0:1, :],
                               1.0 / S, ALU.mult), inc="ve")
                    banks.release(bi, "ve", vv)
                    W("tensor", "sc", p.cnt["sc"])
                    with sbt("OHh", [U1, S], BF16) as OHh, \
                         sbt("osg", [64, 512], BF16) as osg:
                        ostore = [0]
                        prev = [0, 0]
                        for hl in range(GRP):
                            h = g * GRP + hl
                            W("vector", "te", prev[0])
                            vv = V("vector", lambda e, hl=hl:
                                   e.tensor_scalar(
                                       OHh[:, :], iotaTt[:, :],
                                       idxmT[:, hl * 16:hl * 16 + 1],
                                       None, ALU.is_equal), inc="ve")
                            W("tensor", "ve", vv)
                            for c4 in range(NTCH):
                                bi = banks.acquire()
                                bj = banks.acquire()
                                V("tensor",
                                  lambda e, bi=bi, hl=hl, c4=c4:
                                  nc.tensor.matmul(
                                      psb[bi][0:64, :],
                                      ctxn[0:U1,
                                           hl * 65:hl * 65 + 64],
                                      OHh[:, ts(c4, 512)],
                                      start=True, stop=False))
                                tvc = V("tensor",
                                        lambda e, bj=bj, c4=c4:
                                        nc.tensor.matmul(
                                            psb[bj][0:1, :],
                                            onesbf[0:U1, 0:1],
                                            OHh[:, ts(c4, 512)],
                                            start=True, stop=True),
                                        inc="te")
                                W("scalar", "te", tvc)
                                sv = V("scalar", lambda e, bj=bj:
                                       e.copy(rowf[0:1, :],
                                              psb[bj][0:1, :]),
                                       inc="sc")
                                banks.release(bj, "sc", sv)
                                W("vector", "sc", sv)
                                W("vector", "te", prev[0])
                                vv = V("vector", lambda e:
                                       e.tensor_scalar(
                                           rowb[0:1, :], rowf[0:1, :],
                                           -1.0, 1.0, ALU.mult,
                                           ALU.add), inc="ve")
                                W("tensor", "ve", vv)
                                tv = V("tensor",
                                       lambda e, bi=bi, g=g, hl=hl:
                                       nc.tensor.matmul(
                                           psb[bi][0:64, :],
                                           vmb[0:1, g * 512 + hl * 64:
                                               g * 512 + hl * 64 + 64],
                                           rowb[0:1, :],
                                           start=False, stop=True),
                                       inc="te")
                                prev[0] = tv
                                W("scalar", "te", tv)
                                W("scalar", "d_s", ostore[0])
                                sv = V("scalar", lambda e, bi=bi:
                                       e.copy(osg[:, :],
                                              psb[bi][0:64, :]),
                                       inc="sc")
                                banks.release(bi, "sc", sv)
                                W("sync", "sc", sv)
                                ostore[0] = dma(
                                    "sync",
                                    scr["OUTT1"][
                                        (h // 2) * 128
                                        + (h % 2) * 64:
                                        (h // 2) * 128
                                        + (h % 2) * 64 + 64,
                                        c4 * 512:c4 * 512 + 512],
                                    osg[:, :])
            for g in range(NGRP):
                with sbt("QT1", [128, ECH_G * S], BF16) as QT, \
                     sbt("KT1", [128, ECH_G * S], BF16) as KT, \
                     sbt("Vaug1", [128, NT * GRP * 65],
                                    BF16) as Vaug, \
                     sbt("Qtok", [128, NT * 512], BF16) as Qtok, \
                     sbt("Mcols", [128, 128], F32) as Mcols, \
                     sbt("Msum", [128, 128], F32) as Msum:
                    l1_group(g, QT, KT, Vaug, Qtok, Mcols, Msum)
                p.barrier()
            with sbt("wo1", [128, ND * D], BF16) as wo, \
                 sbt("oxr", [128, 2 * ND * 128], BF16) as oxr, \
                 sbt("bor", [1, D], BF16) as bor:
                dv = dma("gpsimd", bor[:, :], w["L1_attn_bo"][None, :])
                for dc in range(ND):
                    dv = dma("gpsimd", wo[:, ts(dc, D)],
                             scr["WO1T"][ts(dc, 128), :])
                W("tensor", "d_g", dv)
                ox_tok = [0, 0]
                for tt in range(NT):
                    xb = tt % 2
                    W("sync", "te", ox_tok[xb])
                    dvx = 0
                    for dc in range(ND):
                        dvx = dma("sync",
                                  oxr[:, xb * ND * 128 + dc * 128:
                                      xb * ND * 128 + dc * 128 + 128],
                                  scr["OUTT1"][ts(dc, 128),
                                               tt * 128:tt * 128 + 128])
                    W("tensor", "d_s", dvx)
                    for ec2 in range(2):
                        bi = banks.acquire()
                        for dc in range(ND):
                            V("tensor",
                              lambda e, dc=dc, bi=bi, xb=xb, ec2=ec2,
                              f=(dc == 0): nc.tensor.matmul(
                                  psb[bi][:, :],
                                  oxr[:, xb * ND * 128 + dc * 128:
                                      xb * ND * 128 + dc * 128 + 128],
                                  wo[:, dc * D + ec2 * 512:
                                     dc * D + ec2 * 512 + 512],
                                  start=f, stop=False))
                        tv = V("tensor", lambda e, bi=bi, ec2=ec2:
                               nc.tensor.matmul(
                                   psb[bi][:, :], onesbf[0:1, 0:128],
                                   bor[0:1, ec2 * 512:ec2 * 512 + 512],
                                   start=False, stop=True), inc="te")
                        ox_tok[xb] = tv
                        W("vector", "te", tv)
                        rsl = res[:, tt * D + ec2 * 512:
                                  tt * D + ec2 * 512 + 512]
                        vv = V("vector", lambda e, bi=bi, rsl=rsl:
                               e.tensor_add(rsl, rsl, psb[bi][:, :]),
                               inc="ve")
                        banks.release(bi, "ve", vv)
        p.barrier()

    def l1_ffn():
        with sbt("convc", [128, NCH * 4], F32) as convc, \
             sbt("xqp", [128, ND * (FFN_TP + 2)], BF16) as xqp, \
             sbt("w1s", [128, 2 * ND * 128], BF16) as w1s, \
             sbt("w2s", [128, 2 * D], BF16) as w2s, \
             sbt("hsb", [128, FFN_TP + 2], F32) as hsb, \
             sbt("hacc", [128, FFN_TP], F32) as hacc, \
             sbt("hact", [128, NCH * FFN_TP], F32) as hact, \
             sbt("hactb", [128, FFN_TP], BF16) as hactb, \
             sbt("hq", [128, NCH * FFN_TP], BF16) as hq, \
             sbt("mx", [128, 8], F32) as mx, \
             sbt("s2r", [1, FFN_TP], F32) as s2r, \
             sbt("s2b", [128, FFN_TP], F32) as s2b:
            for cch in range(NCH):
                dma("sync", convc[:, cch * 4:cch * 4 + 3],
                    w["L1_ffn_conv_w"][ts(cch, 128), 0, :])
                dma("sync", convc[:, cch * 4 + 3:cch * 4 + 4],
                    w["L1_ffn_conv_b"][ts(cch, 128)])
            p.barrier()
            for tp in range(N_FFN_TP):
                W("sync", "te", p.cnt["te"])
                W("vector", "te", p.cnt["te"])
                T0 = tp * FFN_TP
                dvx = 0
                for dc in range(ND):
                    dvx = dma("sync", xqp[:, ts(dc, FFN_TP + 2)],
                              scr["XDQT"][ts(dc, 128), T0:T0 + FFN_TP + 2])
                V("vector", lambda e: e.memset(mx[:, 0:NTL], 0.0), inc="ve")
                w1tok = [0, 0]
                w1tok[0] = 0
                for cch in range(NCH):
                    buf = cch % 2
                    W("gpsimd", "te", p.cnt["te"])
                    dvl = 0
                    for dc in range(ND):
                        dvl = dma("gpsimd",
                                  w1s[:, buf * ND * 128 + dc * 128:
                                      buf * ND * 128 + dc * 128 + 128],
                                  scr["W1QT"][ts(dc, 128), ts(cch, 128)])
                    W("tensor", "d_g", dvl)
                    W("tensor", "d_s", dvx)
                    bi = banks.acquire()
                    tv = None
                    for dc in range(ND):
                        last = dc == ND - 1
                        tv = V("tensor",
                               lambda e, dc=dc, bi=bi, buf=buf, f=(dc == 0),
                               l=last: nc.tensor.matmul(
                                   psb[bi][:, 0:FFN_TP + 2],
                                   w1s[:, buf * ND * 128 + dc * 128:
                                       buf * ND * 128 + dc * 128 + 128],
                                   xqp[:, ts(dc, FFN_TP + 2)],
                                   start=f, stop=l),
                               inc="te" if last else None)
                    W("scalar", "te", tv)
                    W("scalar", "ve", p.cnt["ve"])
                    sv = V("scalar", lambda e, bi=bi: e.activation(
                        hsb[:, :], psb[bi][:, 0:FFN_TP + 2], AF.Copy,
                        scale=colf[:, 40:41]), inc="sc")
                    banks.release(bi, "sc", sv)
                    W("vector", "sc", sv)
                    V("vector", lambda e, cch=cch: e.tensor_scalar(
                        hacc[:, :], hsb[:, 0:FFN_TP],
                        convc[:, cch * 4:cch * 4 + 1], None, ALU.mult),
                        inc="ve")
                    VSP()
                    V("vector", lambda e, cch=cch: e.scalar_tensor_tensor(
                        hacc[:, :], hsb[:, 1:FFN_TP + 1],
                        convc[:, cch * 4 + 1:cch * 4 + 2], hacc[:, :],
                        ALU.mult, ALU.add), inc="ve")
                    VSP()
                    vv = V("vector", lambda e, cch=cch:
                           e.scalar_tensor_tensor(
                               hacc[:, :], hsb[:, 2:FFN_TP + 2],
                               convc[:, cch * 4 + 2:cch * 4 + 3],
                               hacc[:, :], ALU.mult, ALU.add), inc="ve")
                    W("scalar", "ve", vv)
                    sv = V("scalar", lambda e, cch=cch: e.activation(
                        hact[:, ts(cch, FFN_TP)], hacc[:, :], AF.Silu,
                        bias=convc[:, cch * 4 + 3:cch * 4 + 4]), inc="sc")
                    W("vector", "sc", sv)
                    vv = V("vector", lambda e, cch=cch: e.tensor_copy(
                        hactb[:, :], hact[:, ts(cch, FFN_TP)]), inc="ve")
                    W("tensor", "ve", vv)
                    b = tp_state["rr"]
                    tp_state["rr"] = 1 - b
                    tok = tp_state["tok"][b]
                    if tok is not None:
                        p.wait("tensor", tok[0], tok[1])
                    tv = None
                    for tl in range(NTL):
                        tv = V("tensor", lambda e, b=b, tl=tl:
                               nc.tensor.transpose(
                                   psbT[b][:, ts(tl, 128)],
                                   hactb[:, ts(tl, 128)], eye[:, :]),
                               inc="te" if tl == NTL - 1 else None)
                    W("vector", "te", tv)
                    for tl in range(NTL):
                        V("vector", lambda e, b=b, tl=tl: e.tensor_reduce(
                            colf[:, 29:30], psbT[b][:, ts(tl, 128)], AX.X,
                            ALU.max, apply_absolute_value=True), inc="ve")
                        VSP()
                        vv = V("vector", lambda e, tl=tl: e.tensor_max(
                            mx[:, tl:tl + 1], mx[:, tl:tl + 1],
                            colf[:, 29:30]), inc="ve")
                    tp_state["tok"][b] = ("ve", vv)
                VSP()
                V("vector", lambda e: e.tensor_single_scalar(
                    mx[:, 0:NTL], mx[:, 0:NTL], 1e-5, ALU.max), inc="ve")
                VSP()
                V("vector", lambda e: e.reciprocal(
                    mx[:, 4:4 + NTL], mx[:, 0:NTL]), inc="ve")
                VSP()
                vv = V("vector", lambda e: e.tensor_single_scalar(
                    mx[:, 4:4 + NTL], mx[:, 4:4 + NTL], 127.0, ALU.mult),
                    inc="ve")
                vv = V("vector", lambda e: e.memset(junk[:, 0:8], 0.0),
                       inc="ve")
                vv = V("vector", lambda e: e.memset(junk[:, 0:8], 0.0),
                       inc="ve")
                W("tensor", "ve", vv)
                for tl in range(NTL):
                    bi_t = banks.acquire()
                    tvt = V("tensor", lambda e, bi=bi_t, tl=tl:
                            nc.tensor.transpose(
                                psb[bi][0:1, 0:128],
                                mx[:, 4 + tl:5 + tl], eyef[:, :]),
                            inc="te")
                    W("vector", "te", tvt)
                    vvt = V("vector", lambda e, bi=bi_t, tl=tl:
                            e.tensor_copy(s2r[0:1, ts(tl, 128)],
                                          psb[bi][0:1, 0:128]), inc="ve")
                    banks.release(bi_t, "ve", vvt)
                V("vector", lambda e: e.memset(junk[:, 0:8], 0.0), inc="ve")
                V("vector", lambda e: e.memset(junk[:, 0:8], 0.0), inc="ve")
                bcast_row(s2b[:, :], s2r[0:1, :], FFN_TP)
                VSP()
                for cch in range(NCH):
                    V("vector", lambda e, cch=cch: e.tensor_tensor(
                        hacc[:, :], hact[:, ts(cch, FFN_TP)], s2b[:, :],
                        ALU.mult), inc="ve")
                    VSP()
                    V("vector", lambda e: e.tensor_single_scalar(
                        hacc[:, :], hacc[:, :], 128.5, ALU.add), inc="ve")
                    VSP()
                    V("vector", lambda e, cch=cch: e.tensor_copy(
                        hact[:, ts(cch, FFN_TP)], hacc[:, :]), inc="ve")
                    VSP()
                    hrem = hact[:, ts(cch, FFN_TP)]
                    for bit in (128.0, 64.0, 32.0, 16.0, 8.0, 4.0, 2.0,
                                1.0):
                        V("vector", lambda e, bit=bit, hrem=hrem:
                          e.tensor_single_scalar(
                              hsb[:, 0:FFN_TP], hrem, bit, ALU.is_ge),
                          inc="ve")
                        VSP()
                        V("vector", lambda e, bit=bit, hrem=hrem:
                          e.scalar_tensor_tensor(
                              hrem, hsb[:, 0:FFN_TP], -bit, hrem,
                              ALU.mult, ALU.add), inc="ve")
                        VSP()
                    V("vector", lambda e, hrem=hrem: e.tensor_sub(
                        hacc[:, :], hacc[:, :], hrem), inc="ve")
                    VSP()
                    vv = V("vector", lambda e, cch=cch: e.tensor_single_scalar(
                        hq[:, ts(cch, FFN_TP)], hacc[:, :], -128.0,
                        ALU.add), inc="ve")
                W("tensor", "ve", vv)
                dbanks = {}
                tvs = {}
                for tl in range(NTL):
                    for ec2 in range(2):
                        dbanks[(tl, ec2)] = banks.acquire()
                te_after = []
                for cch in range(NCH):
                    buf = cch % 2
                    W("gpsimd", "te",
                      te_after[cch - 2] if cch >= 2 else p.cnt["te"])
                    w2t = dma("gpsimd", w2s[:, ts(buf, D)],
                              scr["W2QT"][ts(cch, 128), :])
                    W("tensor", "d_g", w2t)
                    for tl in range(NTL):
                        for ec2 in range(2):
                            last = cch == NCH - 1
                            fin = tl == NTL - 1 and ec2 == 1
                            tvs[(tl, ec2)] = V(
                                "tensor",
                                lambda e, cch=cch, tl=tl, ec2=ec2, buf=buf,
                                bi=dbanks[(tl, ec2)], f=(cch == 0), l=last:
                                nc.tensor.matmul(
                                    psb[bi][:, :],
                                    hq[:, cch * FFN_TP + tl * 128:
                                       cch * FFN_TP + tl * 128 + 128],
                                    w2s[:, buf * D + ec2 * 512:
                                        buf * D + ec2 * 512 + 512],
                                    start=f, stop=l),
                                inc="te" if (last or fin) else None)
                    te_after.append(p.cnt["te"])
                V("vector", lambda e: e.tensor_scalar(
                    mx[:, 6:6 + NTL], mx[:, 0:NTL], colf[:, 42:43],
                    1.0 / 127.0, ALU.mult, ALU.mult), inc="ve")
                VSP()
                for tl in range(NTL):
                    for ec2 in range(2):
                        W("vector", "te", tvs[(tl, ec2)])
                        ttg = tp * NTL + tl
                        rsl = res[:, ttg * D + ec2 * 512:
                                  ttg * D + ec2 * 512 + 512]
                        vv = V("vector",
                               lambda e, bi=dbanks[(tl, ec2)], rsl=rsl,
                               tl=tl: e.scalar_tensor_tensor(
                                   rsl, psb[bi][:, :],
                                   mx[:, 6 + tl:7 + tl], rsl,
                                   ALU.mult, ALU.add), inc="ve")
                        banks.release(dbanks[(tl, ec2)], "ve", vv)
                p.barrier()
        p.barrier()

    if stage >= 1:
        layernorm("XNT0", xq=False)
    if stage >= 2:
        l0_attention()
    if stage >= 3:
        l0_moe()
    if stage >= 4:
        layernorm("XNT1", xq=True)
    if stage >= 5:
        l1_attention()
    if stage >= 6:
        l1_ffn()

    for tt in range(NT):
        dma("sync", out[ts(tt, 128), :], res[:, ts(tt, D)])
    W("sync", "d_s", p.cnt["d_s"])
    p.build_block()
    p.stack.close()
    return nc, p


def kernel(**inputs):
    x = np.ascontiguousarray(inputs["L0_attn_x"], dtype=np.float32)
    nc = build_nc()
    shared = {k: np.ascontiguousarray(v) for k, v in inputs.items()
              if k != "L0_attn_x"}
    in_maps = []
    for c in range(8):
        m = dict(shared)
        m["x"] = np.ascontiguousarray(x[c])
        in_maps.append(m)
    r = run_bass_kernel_spmd(nc, in_maps, core_ids=list(range(8)))
    return np.stack([rr["out"] for rr in r.results], axis=0)


# version 03
# speedup vs baseline: 1.0005x; 1.0005x over previous
import math
from contextlib import ExitStack

import numpy as np

import concourse.bass as bass
import concourse.mybir as mybir
from concourse.bass_utils import run_bass_kernel_spmd

F32 = mybir.dt.float32
BF16 = mybir.dt.bfloat16
FP16 = mybir.dt.float16
I32 = mybir.dt.int32
AF = mybir.ActivationFunctionType
ALU = mybir.AluOpType
AX = mybir.AxisListType

S = 2048
D = 1024
H = 16
NE = 8
FF = 2816
DH = 4096
NT = S // 128
ND = D // 128
NTCH = S // 512
U1 = max(1, min(S, int(5.0 * math.log(S + 1))))
TWO_PI = 2.0 * math.pi
GRP = 8
NGRP = H // GRP
ECH_G = GRP * 64 // 128
MOE_TP = 1024
N_MOE_TP = S // MOE_TP
NFG = FF // 128
FFN_TP = 256
N_FFN_TP = S // FFN_TP
NCH = DH // 128
NTL = FFN_TP // 128

ENGINES = ("sync", "tensor", "vector", "scalar", "gpsimd")


SETTLE_SEMS = ("ve", "sc", "gp")


class Prog:
    def __init__(self, nc, waited=None):
        self.nc = nc
        self.q = {e: [] for e in ENGINES}
        self.sems = {}
        self.cnt = {}
        self.stack = ExitStack()
        self.record = waited is None
        self.waited = waited if waited is not None else {s: set()
                                                         for s in SETTLE_SEMS}
        self.real_ord = {s: 0 for s in SETTLE_SEMS}
        self.settle_fn = {}

    def new_sem(self, name):
        h = self.stack.enter_context(self.nc.semaphore(name))
        self.sems[name] = h
        self.cnt[name] = 0

    def emit(self, e, fn):
        self.q[e].append(fn)

    def wait(self, e, name, val):
        if not val:
            return
        if self.record and name in self.waited:
            self.waited[name].add(val)
        h = self.sems[name]
        self.emit(e, lambda eng, h=h, v=val: eng.wait_ge(h, v))
        if name == "te" and e in ("vector", "scalar") and e in self.settle_fn:
            sf = self.settle_fn[e]
            self.emit(e, lambda eng, sf=sf: sf(eng))

    def run(self, e, fn, inc=None, n=1):
        if inc is None:
            self.emit(e, fn)
            return None
        h = self.sems[inc]
        if inc in self.real_ord:
            self.real_ord[inc] += 1
            if (not self.record) and self.real_ord[inc] in self.waited[inc] \
                    and e in self.settle_fn:
                self.cnt[inc] += n
                self.emit(e, lambda eng, f=fn: f(eng))
                sf = self.settle_fn[e]
                self.emit(e, lambda eng, sf=sf: sf(eng))
                self.emit(e, lambda eng, sf=sf, h=h, n=n:
                          sf(eng).then_inc(h, n))
                return self.cnt[inc]
        self.cnt[inc] += n
        self.emit(e, lambda eng, f=fn, h=h, n=n: f(eng).then_inc(h, n))
        if inc == "ve" and e == "vector" and "vector" in self.settle_fn:
            sf = self.settle_fn["vector"]
            self.emit(e, lambda eng, sf=sf: sf(eng))
        return self.cnt[inc]

    def barrier(self):
        for e in ENGINES:
            if e in self.settle_fn:
                sf = self.settle_fn[e]
                self.emit(e, lambda eng, sf=sf: sf(eng))
                self.emit(e, lambda eng, sf=sf: sf(eng))
            self.cnt["b_" + e] += 1
            h = self.sems["b_" + e]
            self.emit(e, lambda eng, h=h: eng.sem_inc(h, 1))
        snap = {e: self.cnt["b_" + e] for e in ENGINES}
        dma_snap = {k: self.cnt[k] for k in self.sems if k.startswith("d_")}
        for e in ENGINES:
            for e2 in ENGINES:
                if e2 != e:
                    self.wait(e, "b_" + e2, snap[e2])
            for k, v in dma_snap.items():
                self.wait(e, k, v)

    def build_block(self):
        with self.nc.Block() as blk:
            @blk.sync
            def _(eng):
                for f in self.q["sync"]:
                    f(eng)

            @blk.tensor
            def _(eng):
                for f in self.q["tensor"]:
                    f(eng)

            @blk.vector
            def _(eng):
                for f in self.q["vector"]:
                    f(eng)

            @blk.scalar
            def _(eng):
                for f in self.q["scalar"]:
                    f(eng)

            @blk.gpsimd
            def _(eng):
                for f in self.q["gpsimd"]:
                    f(eng)


class Banks:
    def __init__(self, prog, handles):
        self.p = prog
        self.h = handles
        self.free_tok = [None] * len(handles)
        self.rr = 0

    def acquire(self, use_on="tensor"):
        i = self.rr
        self.rr = (self.rr + 1) % len(self.h)
        tok = self.free_tok[i]
        if tok is not None:
            self.p.wait(use_on, tok[0], tok[1])
            self.free_tok[i] = None
        return i

    def release(self, i, semname, val):
        self.free_tok[i] = (semname, val)


def ts(i, n):
    return slice(i * n, (i + 1) * n)


def build_nc(stage=9):
    nc, p = _build_nc(stage, None)
    return _build_nc(stage, p.waited)[0]


def _build_nc(stage=9, _waited=None):
    nc = bass.Bass("TRN2", target_bir_lowering=False,
                   detect_race_conditions=False)
    p = Prog(nc, _waited)
    st = p.stack
    V, W = p.run, p.wait
    _ctr = [0]

    def sbt(name, shape, dtype):
        _ctr[0] += 1
        return nc.sbuf_tensor(f"{name}_{_ctr[0]}", shape, dtype)

    xin = nc.dram_tensor("x", [S, D], F32, kind="ExternalInput").ap()
    w = {}
    for name, shape in [
        ("L0_attn_W_Q", [D, D]), ("L0_attn_W_K", [D, D]),
        ("L0_attn_W_V", [D, D]), ("L0_attn_W_O", [D, D]),
        ("L0_ffn_router_w", [D, NE]),
        ("L0_ffn_gate_proj", [NE, D, FF]), ("L0_ffn_up_proj", [NE, D, FF]),
        ("L0_ffn_down_proj", [NE, FF, D]),
        ("L1_attn_Wq", [D, D]), ("L1_attn_Wk", [D, D]),
        ("L1_attn_Wv", [D, D]), ("L1_attn_Wo", [D, D]),
        ("L1_attn_bq", [D]), ("L1_attn_bk", [D]),
        ("L1_attn_bv", [D]), ("L1_attn_bo", [D]),
        ("L1_ffn_w1", [DH, D]), ("L1_ffn_conv_w", [DH, 1, 3]),
        ("L1_ffn_conv_b", [DH]), ("L1_ffn_w2", [D, DH]),
    ]:
        w[name] = nc.dram_tensor(name, shape, F32, kind="ExternalInput").ap()
    w["L1_attn_sample_idx"] = nc.dram_tensor(
        "L1_attn_sample_idx", [S], I32, kind="ExternalInput").ap()
    out = nc.dram_tensor("out", [S, D], F32, kind="ExternalOutput").ap()

    scr = {}
    for nm in ["WQ0T", "WK0T", "WV0T", "WO0T", "WQ1T", "WK1T", "WV1T",
               "WO1T"]:
        scr[nm] = nc.dram_tensor(nm, [D, D], BF16, kind="Internal").ap()
    scr["W1QT"] = nc.dram_tensor("W1QT", [D, DH], BF16, kind="Internal").ap()
    scr["W2QT"] = nc.dram_tensor("W2QT", [DH, D], BF16, kind="Internal").ap()
    scr["XDQT"] = nc.dram_tensor("XDQT", [D, S + 2], BF16,
                                 kind="Internal").ap()
    scr["XNT0"] = nc.dram_tensor("XNT0", [D, S], BF16, kind="Internal").ap()
    scr["XNT1"] = nc.dram_tensor("XNT1", [D, S], BF16, kind="Internal").ap()
    scr["OUTT1"] = nc.dram_tensor("OUTT1", [D, S], BF16,
                                  kind="Internal").ap()

    for e in ENGINES:
        p.new_sem("b_" + e)
    for nm in ["te", "ve", "sc", "gp", "d_s", "d_g", "d_c", "d_v"]:
        p.new_sem(nm)

    res = st.enter_context(sbt("res", [128, NT * D], F32))
    junk = st.enter_context(sbt("junk", [128, 8], F32))
    junkB = st.enter_context(sbt("junkB", [128, 8], F32))
    junkG = st.enter_context(sbt("junkG", [128, 8], F32))
    junkW = st.enter_context(sbt("junkW", [128, 512], F32))
    p.settle_fn = {
        "vector": lambda eng: eng.memset(junk[:, 0:8], 0.0),
        "scalar": lambda eng: eng.copy(junkB[:, 0:8], junk[:, 0:8]),
        "gpsimd": lambda eng: eng.iota(
            junkG[:, 0:1], [[0, 1]], base=0, channel_multiplier=1,
            allow_small_or_imprecise_dtypes=True),
    }

    def VSP():
        p.emit("vector", lambda e: e.memset(junk[:, 0:8], 0.0))
    eye = st.enter_context(sbt("eye", [128, 128], BF16))
    eyef = st.enter_context(sbt("eyef", [128, 128], F32))
    onesf = st.enter_context(sbt("onesf", [1, 128], F32))
    onesbf = st.enter_context(sbt("onesbf", [128, 512], BF16))
    colf = st.enter_context(sbt("colf", [128, 64], F32))
    rowf = st.enter_context(sbt("rowf", [1, 512], F32))
    rowb = st.enter_context(sbt("rowb", [1, 512], BF16))
    tmpA = st.enter_context(sbt("tmpA", [128, 512], F32))
    tmpB = st.enter_context(sbt("tmpB", [128, 512], F32))
    tmpC = st.enter_context(sbt("tmpC", [128, 512], F32))
    sqb = st.enter_context(sbt("sqb", [128, 512], BF16))
    iotaT16 = st.enter_context(sbt("iotaT16", [128, NT], F32))
    cst = st.enter_context(sbt("cst", [128, 2], F32))
    psb = [st.enter_context(nc.psum_tensor(f"pb{i}", [128, 512], F32))
           for i in range(6)]
    banks = Banks(p, psb)
    psbT = [st.enter_context(nc.psum_tensor(f"pt{i}", [128, 1024], BF16))
            for i in range(2)]
    tp_state = {"rr": 0, "tok": [None, None]}

    def transpose_batch(srcs, dst):
        b = tp_state["rr"]
        tp_state["rr"] = 1 - b
        tok = tp_state["tok"][b]
        if tok is not None:
            p.wait("tensor", tok[0], tok[1])
        W("tensor", "ve", p.cnt["ve"])
        W("tensor", "sc", p.cnt["sc"])
        tv = None
        n = len(srcs)
        for i, s in enumerate(srcs):
            tv = V("tensor", lambda e, s=s, i=i, b=b: nc.tensor.transpose(
                psbT[b][:, ts(i, 128)], s, eye[:, :]),
                inc="te" if i == n - 1 else None)
        W("scalar", "te", tv)
        sv = V("scalar", lambda e, b=b, n=n, dst=dst: e.copy(
            dst, psbT[b][:, 0:n * 128]), inc="sc")
        tp_state["tok"][b] = ("sc", sv)
        return sv

    def bcast_row(dst, src_row, n, npart=128):
        last = 0
        W("tensor", "ve", p.cnt["ve"])
        W("tensor", "d_s", p.cnt["d_s"])
        for c0 in range(0, n, 512):
            cw = min(512, n - c0)
            bi = banks.acquire()
            tv = V("tensor", lambda e, bi=bi, c0=c0, cw=cw, npart=npart:
                   nc.tensor.matmul(psb[bi][0:npart, 0:cw],
                                    onesf[0:1, 0:npart],
                                    src_row[0:1, c0:c0 + cw],
                                    start=True, stop=True), inc="te")
            W("vector", "te", tv)
            last = V("vector", lambda e, bi=bi, c0=c0, cw=cw, npart=npart:
                     e.tensor_copy(dst[0:npart, c0:c0 + cw],
                                   psb[bi][0:npart, 0:cw]), inc="ve")
            banks.release(bi, "ve", last)
        return last

    def col_mod(dst, src_col, m, tmp1, tmp2):
        V("vector", lambda e: e.tensor_single_scalar(
            tmp1, src_col, 1.0 / m, ALU.mult), inc="ve")
        VSP()
        for bit in (256.0, 128.0, 64.0, 32.0, 16.0, 8.0, 4.0, 2.0, 1.0):
            V("vector", lambda e, bit=bit: e.tensor_single_scalar(
                tmp2, tmp1, bit, ALU.is_ge), inc="ve")
            VSP()
            V("vector", lambda e, bit=bit: e.scalar_tensor_tensor(
                tmp1, tmp2, -bit, tmp1, ALU.mult, ALU.add), inc="ve")
            VSP()
        return V("vector", lambda e: e.tensor_single_scalar(
            dst, tmp1, float(m), ALU.mult), inc="ve")

    def emit_round(dst, src_ap, scale_col, post_add, post_mul_col, tmpF,
                   tmpR, tmpC, n):
        V("vector", lambda e: e.tensor_scalar(
            tmpF, src_ap, scale_col, 128.5, ALU.mult, ALU.add), inc="ve")
        VSP()
        V("vector", lambda e: e.tensor_copy(tmpR, tmpF), inc="ve")
        VSP()
        for bit in (128.0, 64.0, 32.0, 16.0, 8.0, 4.0, 2.0, 1.0):
            V("vector", lambda e, bit=bit: e.tensor_single_scalar(
                tmpC, tmpR, bit, ALU.is_ge), inc="ve")
            VSP()
            V("vector", lambda e, bit=bit: e.scalar_tensor_tensor(
                tmpR, tmpC, -bit, tmpR, ALU.mult, ALU.add), inc="ve")
            VSP()
        V("vector", lambda e: e.tensor_sub(tmpF, tmpF, tmpR), inc="ve")
        VSP()
        return V("vector", lambda e: e.tensor_scalar(
            dst, tmpF, -128.0 + post_add, post_mul_col, ALU.add, ALU.mult),
            inc="ve")

    def dma(queue, dst, src, sem=None, slow_ok=False):
        sem = sem or ("d_g" if queue == "gpsimd" else
                      "d_c" if queue == "scalar" else
                      "d_v" if queue == "vector" else "d_s")
        return p.run(queue, lambda e, d=dst, s=src, so=slow_ok: e.dma_start(
            out=d, in_=s, allow_slow_non_contiguous=so), inc=sem, n=16)

    V("gpsimd", lambda e: e.iota(tmpA[:, 0:128], [[-1, 128]], base=0,
                                 channel_multiplier=1, allow_small_or_imprecise_dtypes=True), inc="gp")
    V("gpsimd", lambda e: e.iota(colf[:, 0:1], [[0, 1]], base=0,
                                 channel_multiplier=1, allow_small_or_imprecise_dtypes=True), inc="gp")
    gv = V("gpsimd", lambda e: e.iota(iotaT16[:, :], [[128, NT]], base=0,
                                      channel_multiplier=1, allow_small_or_imprecise_dtypes=True), inc="gp")
    W("vector", "gp", gv)
    V("vector", lambda e: e.tensor_single_scalar(
        eye[:, :], tmpA[:, 0:128], 0.0, ALU.is_equal), inc="ve")
    V("vector", lambda e: e.tensor_single_scalar(
        eyef[:, :], tmpA[:, 0:128], 0.0, ALU.is_equal), inc="ve")
    V("vector", lambda e: e.memset(onesbf[:, :], 1.0), inc="ve")
    V("vector", lambda e: e.memset(onesf[:, :], 1.0), inc="ve")
    V("vector", lambda e: e.memset(cst[:, 0:1], 1e-5), inc="ve")
    V("vector", lambda e: e.memset(cst[:, 1:2], 0.125), inc="ve")
    p.barrier()

    def prep_wT(src_ap, dst_ap):
        with sbt("pw_in", [128, D], BF16) as win, \
             sbt("pw_out", [128, D], BF16) as wout:
            last_store = [0]
            for ec in range(ND):
                W("gpsimd", "te", p.cnt["te"])
                dv = dma("gpsimd", win[:, :], src_ap[ts(ec, 128), :])
                W("tensor", "d_g", dv)
                W("scalar", "d_s", last_store[0])
                sv = transpose_batch([win[:, ts(dc, 128)]
                                      for dc in range(ND)], wout[:, :])
                W("sync", "sc", sv)
                for dc in range(ND):
                    last_store[0] = dma("sync",
                                        dst_ap[ts(dc, 128), ts(ec, 128)],
                                        wout[:, ts(dc, 128)])

    def prep_cast(src_ap, dst_ap):
        with sbt("pc_in", [128, D], BF16) as win:
            for rc in range(ND):
                W("gpsimd", "d_s", p.cnt["d_s"])
                dv = dma("gpsimd", win[:, :], src_ap[ts(rc, 128), :])
                W("sync", "d_g", dv)
                dma("sync", dst_ap[ts(rc, 128), :], win[:, :])

    for nm, srcnm in [("WQ0T", "L0_attn_W_Q"), ("WK0T", "L0_attn_W_K"),
                      ("WV0T", "L0_attn_W_V"), ("WO0T", "L0_attn_W_O")]:
        prep_wT(w[srcnm], scr[nm])
        p.barrier()
    for nm, srcnm in [("WQ1T", "L1_attn_Wq"), ("WK1T", "L1_attn_Wk"),
                      ("WV1T", "L1_attn_Wv"), ("WO1T", "L1_attn_Wo")]:
        prep_cast(w[srcnm], scr[nm])
        p.barrier()

    def prep_wq(src_ap, dst_ap, n_r, n_c, slot):
        nrc, ncc = n_r // 128, n_c // 128
        with sbt("q_in", [128, n_c], F32) as qin, \
             sbt("q_f", [128, n_c], F32) as qf, \
             sbt("q_t", [128, n_c], BF16) as qt, \
             sbt("q_acc", [128, 2], F32) as qacc, \
             sbt("q_ones", [128, 1], F32) as qones, \
             sbt("q_ev", [128, 1024], BF16) as qev:
            V("vector", lambda e: e.memset(qacc[:, 0:1], 0.0), inc="ve")
            V("vector", lambda e: e.memset(qones[:, :], 1.0), inc="ve")
            vv = p.cnt["ve"]
            for rc in range(nrc):
                W("sync", "ve", p.cnt["ve"])
                dv = dma("sync", qin[:, :], src_ap[ts(rc, 128), :])
                W("vector", "d_s", dv)
                V("vector", lambda e: e.tensor_reduce(
                    qacc[:, 1:2], qin[:, :], AX.X, ALU.add,
                    apply_absolute_value=True), inc="ve")
                VSP()
                vv = V("vector", lambda e: e.tensor_add(
                    qacc[:, 0:1], qacc[:, 0:1], qacc[:, 1:2]), inc="ve")
            vv = V("vector", lambda e: e.memset(junk[:, 0:8], 0.0), inc="ve")
            vv = V("vector", lambda e: e.memset(junk[:, 0:8], 0.0), inc="ve")
            W("tensor", "ve", vv)
            bi = banks.acquire()
            tv = V("tensor", lambda e, bi=bi: nc.tensor.matmul(
                psb[bi][0:1, 0:1], qones[:, :], qacc[:, 0:1],
                start=True, stop=True), inc="te")
            W("vector", "te", tv)
            V("vector", lambda e, bi=bi: e.tensor_single_scalar(
                colf[0:1, 60:61], psb[bi][0:1, 0:1], 1.0 / (n_r * n_c),
                ALU.mult), inc="ve")
            VSP()
            VSP()
            V("vector", lambda e: e.tensor_single_scalar(
                colf[0:1, 60:61], colf[0:1, 60:61], 1e-5, ALU.max), inc="ve")
            VSP()
            VSP()
            vv = V("vector", lambda e: e.reciprocal(
                colf[0:1, 61:62], colf[0:1, 60:61]), inc="ve")
            vv = V("vector", lambda e: e.memset(junk[:, 0:8], 0.0), inc="ve")
            vv = V("vector", lambda e: e.memset(junk[:, 0:8], 0.0), inc="ve")
            banks.release(bi, "ve", vv)
            bcast_row(colf[:, slot:slot + 1], colf[0:1, 60:61], 1)
            bcast_row(colf[:, slot + 1:slot + 2], colf[0:1, 61:62], 1)
            last_store = [0]
            for rc in range(nrc):
                W("sync", "ve", p.cnt["ve"])
                dv = dma("sync", qin[:, :], src_ap[ts(rc, 128), :])
                W("vector", "d_s", dv)
                W("vector", "te", p.cnt["te"])
                V("vector", lambda e: e.tensor_scalar(
                    qf[:, :], qin[:, :], colf[:, slot + 1:slot + 2], None,
                    ALU.mult), inc="ve")
                VSP()
                V("vector", lambda e: e.tensor_single_scalar(
                    qin[:, :], qf[:, :], 0.5, ALU.is_ge), inc="ve")
                V("vector", lambda e: e.tensor_single_scalar(
                    qf[:, :], qf[:, :], -0.5, ALU.is_le), inc="ve")
                VSP()
                vv = V("vector", lambda e: e.tensor_tensor(
                    qt[:, :], qin[:, :], qf[:, :], ALU.subtract), inc="ve")
                W("tensor", "ve", vv)
                for cb in range(ncc // 8):
                    W("scalar", "d_s", last_store[0])
                    sv = transpose_batch(
                        [qt[:, ts(cb * 8 + i, 128)] for i in range(8)],
                        qev[:, 0:1024])
                    W("sync", "sc", sv)
                    for i in range(8):
                        last_store[0] = dma(
                            "sync", dst_ap[ts(cb * 8 + i, 128), ts(rc, 128)],
                            qev[:, ts(i, 128)])

    prep_wq(w["L1_ffn_w1"], scr["W1QT"], DH, D, 40)
    p.barrier()
    prep_wq(w["L1_ffn_w2"], scr["W2QT"], D, DH, 42)
    p.barrier()

    for tt in range(NT):
        dma("sync", res[:, ts(tt, D)], xin[ts(tt, 128), :])
    p.barrier()

    def layernorm(xnt_name, xq=False):
        with sbt("ln_xn", [128, D], F32) as xn_tok, \
             sbt("ln_f", [128, D], F32) as lf, \
             sbt("ln_bf", [128, D], BF16) as xn_bf, \
             sbt("ln_q", [128, D], BF16) as xq_bf, \
             sbt("ln_qa", [128, D], F32) as qtA, \
             sbt("ln_qb", [128, D], F32) as qtB, \
             sbt("ln_ev", [128, 8 * 128], BF16) as lev:
            last_store = [0]
            for tt in range(NT):
                rsl = res[:, ts(tt, D)]
                W("vector", "te", p.cnt["te"])
                W("vector", "sc", p.cnt["sc"])
                W("scalar", "ve", p.cnt["ve"])
                V("vector", lambda e, rsl=rsl: e.tensor_reduce(
                    colf[:, 1:2], rsl, AX.X, ALU.add), inc="ve")
                sv = V("scalar", lambda e, rsl=rsl: e.activation(
                    lf[:, :], rsl, AF.Square, accum_out=colf[:, 2:3]),
                    inc="sc")
                W("vector", "sc", sv)
                V("vector", lambda e: e.tensor_single_scalar(
                    colf[:, 1:2], colf[:, 1:2], 1.0 / D, ALU.mult), inc="ve")
                V("vector", lambda e: e.tensor_single_scalar(
                    colf[:, 2:3], colf[:, 2:3], 1.0 / D, ALU.mult), inc="ve")
                V("vector", lambda e: e.tensor_tensor(
                    colf[:, 3:4], colf[:, 1:2], colf[:, 1:2], ALU.mult),
                    inc="ve")
                VSP()
                vv = V("vector", lambda e: e.tensor_sub(
                    colf[:, 2:3], colf[:, 2:3], colf[:, 3:4]), inc="ve")
                W("scalar", "ve", vv)
                sv = V("scalar", lambda e: e.activation(
                    colf[:, 3:4], colf[:, 2:3], AF.Sqrt, bias=cst[:, 0:1]), inc="sc")
                W("vector", "sc", sv)
                V("vector", lambda e: e.reciprocal(
                    colf[:, 3:4], colf[:, 3:4]), inc="ve")
                VSP()
                vv = V("vector", lambda e, rsl=rsl: e.tensor_scalar(
                    xn_tok[:, :], rsl, colf[:, 1:2], colf[:, 3:4],
                    ALU.subtract, ALU.mult), inc="ve")
                VSP()
                V("vector", lambda e: e.tensor_copy(
                    xn_bf[:, :], xn_tok[:, :]), inc="ve")
                if xq:
                    V("vector", lambda e: e.tensor_reduce(
                        colf[:, 4:5], xn_tok[:, :], AX.X, ALU.max,
                        apply_absolute_value=True), inc="ve")
                    VSP()
                    V("vector", lambda e: e.tensor_single_scalar(
                        colf[:, 4:5], colf[:, 4:5], 1e-5, ALU.max), inc="ve")
                    VSP()
                    V("vector", lambda e: e.reciprocal(
                        colf[:, 5:6], colf[:, 4:5]), inc="ve")
                    VSP()
                    V("vector", lambda e: e.tensor_single_scalar(
                        colf[:, 5:6], colf[:, 5:6], 127.0, ALU.mult),
                        inc="ve")
                    V("vector", lambda e: e.tensor_single_scalar(
                        colf[:, 6:7], colf[:, 4:5], 1.0 / 127.0, ALU.mult),
                        inc="ve")
                    vv = emit_round(xq_bf[:, :], xn_tok[:, :],
                                    colf[:, 5:6], 0.0, colf[:, 6:7],
                                    lf[:, :], qtA[:, 0:D], qtB[:, 0:D], D)
                W("tensor", "ve", vv)
                srcs = [(xn_bf, scr[xnt_name], 0)]
                if xq:
                    srcs.append((xq_bf, scr["XDQT"], 1))
                for sb, tgt, off in srcs:
                    W("scalar", "d_s", last_store[0])
                    sv = transpose_batch([sb[:, ts(dc, 128)]
                                          for dc in range(ND)], lev[:, :])
                    W("sync", "sc", sv)
                    for dc in range(ND):
                        cpos = tt * 128 + off
                        last_store[0] = dma(
                            "sync", tgt[ts(dc, 128), cpos:cpos + 128],
                            lev[:, ts(dc, 128)])
        if xq:
            with sbt("zc", [128, 1], BF16) as zc:
                vv = V("vector", lambda e: e.memset(zc[:, :], 0.0), inc="ve")
                W("sync", "ve", vv)
                for dc in range(ND):
                    dma("sync", scr["XDQT"][ts(dc, 128), 0:1], zc[:, :],
                        slow_ok=True)
                    dma("sync", scr["XDQT"][ts(dc, 128), S + 1:S + 2],
                        zc[:, :], slow_ok=True)
        p.barrier()

    def qk_bound(QT, KT, negc_slot, scale):
        p.barrier()
        with sbt("bn_ones", [128, 2], BF16) as bones, \
             sbt("bn_acc", [32, 2], F32) as bacc, \
             sbt("bn_sh", [32, 2], F32) as bsh:
            V("vector", lambda e: e.memset(bones[:, :], 0.0), inc="ve")
            V("vector", lambda e: e.memset(bones[0:64, 0:1], 1.0), inc="ve")
            V("vector", lambda e: e.memset(bones[64:128, 1:2], 1.0),
              inc="ve")
            vv = V("vector", lambda e: e.memset(bacc[:, :], 0.0), inc="ve")
            W("scalar", "ve", vv)
            for qi, srcT in ((0, QT), (1, KT)):
                for ec in range(ECH_G):
                    for tc in range(NTCH):
                        W("scalar", "te", p.cnt["te"])
                        sv = V("scalar", lambda e, ec=ec, tc=tc, srcT=srcT:
                               e.activation(
                                   sqb[:, :],
                                   srcT[:, ec * S + tc * 512:
                                        ec * S + tc * 512 + 512],
                                   AF.Square), inc="sc")
                        W("tensor", "sc", sv)
                        bi = banks.acquire()
                        tv = V("tensor", lambda e, bi=bi: nc.tensor.matmul(
                            psb[bi][0:2, :], bones[:, :], sqb[:, :],
                            start=True, stop=True), inc="te")
                        W("vector", "te", tv)
                        V("vector", lambda e, bi=bi: e.tensor_reduce(
                            colf[0:2, 12:13], psb[bi][0:2, :], AX.X,
                            ALU.max), inc="ve")
                        vv = V("vector", lambda e, qi=qi: e.tensor_max(
                            bacc[0:2, qi:qi + 1], bacc[0:2, qi:qi + 1],
                            colf[0:2, 12:13]), inc="ve")
                        banks.release(bi, "ve", vv)
            V("vector", lambda e: e.stream_shuffle(
                bsh[:, :], bacc[:, :], [i ^ 1 for i in range(32)]), inc="ve")
            vv = V("vector", lambda e: e.tensor_max(
                bacc[:, :], bacc[:, :], bsh[:, :]), inc="ve")
            W("scalar", "ve", vv)
            sv = V("scalar", lambda e: e.activation(
                bacc[0:1, :], bacc[0:1, :], AF.Sqrt), inc="sc")
            W("vector", "sc", sv)
            V("vector", lambda e: e.tensor_tensor(
                colf[0:1, 14:15], bacc[0:1, 0:1], bacc[0:1, 1:2], ALU.mult),
                inc="ve")
            vv = V("vector", lambda e, scale=scale: e.tensor_single_scalar(
                colf[0:1, 14:15], colf[0:1, 14:15], -scale, ALU.mult),
                inc="ve")
            lastb = bcast_row(colf[:, negc_slot:negc_slot + 1],
                              colf[0:1, 14:15], 1)
            W("scalar", "ve", lastb)

    def proj_feature(dstT, wT_name, xnt_scr, group, bias_src=None,
                     rope=None):
        p.barrier()
        with sbt("pf_l", [128, 2 * ND * 128], BF16) as lr, \
             sbt("pf_x", [128, ND * 512], BF16) as xr, \
             sbt("pf_b", [1, D], BF16) as brr:
            if bias_src is not None:
                dvb = dma("gpsimd", brr[:, :], bias_src[None, :])
                W("tensor", "d_g", dvb)
            lr_tok = {}
            seq = [(tc, ec) for tc in range(NTCH) for ec in range(ECH_G)]

            def load_l(ec, buf):
                dv = 0
                eg = group * ECH_G + ec
                for dc in range(ND):
                    dv = dma("gpsimd",
                             lr[:, buf * ND * 128 + dc * 128:
                                buf * ND * 128 + dc * 128 + 128],
                             scr[wT_name][ts(dc, 128), ts(eg, 128)])
                return dv

            lr_tok[0] = load_l(0, 0)
            xr_done = 0
            for si, (tc, ec) in enumerate(seq):
                buf = si % 2
                if ec == 0:
                    W("sync", "te", xr_done)
                    dvx = 0
                    for dc in range(ND):
                        dvx = dma("sync", xr[:, ts(dc, 512)],
                                  xnt_scr[ts(dc, 128), ts(tc, 512)])
                if si + 1 < len(seq):
                    W("gpsimd", "te", p.cnt["te"])
                    lr_tok[1 - buf] = load_l(seq[si + 1][1], 1 - buf)
                W("tensor", "d_g", lr_tok[buf])
                W("tensor", "d_s", dvx)
                bi = banks.acquire()
                tv = None
                for dc in range(ND):
                    last = dc == ND - 1
                    lstop = last and bias_src is None
                    tv = V("tensor",
                           lambda e, dc=dc, bi=bi, buf=buf, f=(dc == 0),
                           l=lstop: nc.tensor.matmul(
                               psb[bi][:, :],
                               lr[:, buf * ND * 128 + dc * 128:
                                  buf * ND * 128 + dc * 128 + 128],
                               xr[:, ts(dc, 512)], start=f, stop=l),
                           inc="te" if last else None)
                if bias_src is not None:
                    eg = group * ECH_G + ec
                    tv = V("tensor", lambda e, bi=bi, eg=eg:
                           nc.tensor.matmul(
                               psb[bi][:, :],
                               brr[0:1, eg * 128:eg * 128 + 128],
                               onesbf[0:1, :], start=False, stop=True),
                           inc="te")
                xr_done = tv
                dst_sl = dstT[:, ec * S + tc * 512:ec * S + tc * 512 + 512]
                if rope is None:
                    W("scalar", "te", tv)
                    sv = V("scalar", lambda e, bi=bi, dst_sl=dst_sl: e.copy(
                        dst_sl, psb[bi][:, :]), inc="sc")
                    banks.release(bi, "sc", sv)
                else:
                    ropeC, ropeS = rope
                    W("scalar", "te", tv)
                    W("scalar", "ve", p.cnt["ve"])
                    sv = V("scalar", lambda e, bi=bi: e.copy(
                        tmpA[:, :], psb[bi][:, :]), inc="sc")
                    W("vector", "sc", sv)
                    V("vector", lambda e: e.stream_shuffle(
                        tmpB[:, :], tmpA[:, :], [i ^ 1 for i in range(32)]),
                        inc="ve")
                    V("vector", lambda e, tc=tc: e.tensor_tensor(
                        tmpC[:, :], tmpA[:, :], ropeC[:, ts(tc, 512)],
                        ALU.mult), inc="ve")
                    V("vector", lambda e, tc=tc: e.tensor_tensor(
                        tmpB[:, :], tmpB[:, :], ropeS[:, ts(tc, 512)],
                        ALU.mult), inc="ve")
                    vv = V("vector", lambda e, dst_sl=dst_sl: e.tensor_add(
                        dst_sl, tmpC[:, :], tmpB[:, :]), inc="ve")
                    banks.release(bi, "ve", vv)

    def proj_token(vaug, wT_name, xnt_scr, group, bias_src=None):
        p.barrier()
        vv = V("vector", lambda e: e.memset(
            vaug[:, :].rearrange("p (t h c) -> p t h c", h=GRP, c=65)
            [:, :, :, 64:65], 1.0), inc="ve")
        W("scalar", "ve", vv)
        with sbt("pt_w", [128, ND * 512], BF16) as vr, \
             sbt("pt_x", [128, 2 * ND * 128], BF16) as xr, \
             sbt("pt_b", [1, D], BF16) as brr:
            if bias_src is not None:
                dvb = dma("gpsimd", brr[:, :], bias_src[None, :])
                W("tensor", "d_g", dvb)
            dvl = 0
            for dc in range(ND):
                dvl = dma("gpsimd", vr[:, ts(dc, 512)],
                          scr[wT_name][ts(dc, 128), ts(group, 512)])
            W("tensor", "d_g", dvl)
            xr_tok = [0, 0]
            for tt in range(NT):
                xb = tt % 2
                W("sync", "te", xr_tok[xb])
                dvx = 0
                for dc in range(ND):
                    dvx = dma("sync", xr[:, xb * ND * 128 + dc * 128:
                                         xb * ND * 128 + dc * 128 + 128],
                              xnt_scr[ts(dc, 128), tt * 128:tt * 128 + 128])
                W("tensor", "d_s", dvx)
                bi = banks.acquire()
                tv = None
                for dc in range(ND):
                    last = dc == ND - 1
                    lstop = last and bias_src is None
                    tv = V("tensor", lambda e, dc=dc, bi=bi, xb=xb,
                           f=(dc == 0), l=lstop: nc.tensor.matmul(
                               psb[bi][:, :],
                               xr[:, xb * ND * 128 + dc * 128:
                                  xb * ND * 128 + dc * 128 + 128],
                               vr[:, ts(dc, 512)], start=f, stop=l),
                           inc="te" if last else None)
                if bias_src is not None:
                    tv = V("tensor", lambda e, bi=bi: nc.tensor.matmul(
                        psb[bi][:, :], onesbf[0:1, 0:128],
                        brr[0:1, group * 512:group * 512 + 512],
                        start=False, stop=True), inc="te")
                xr_tok[xb] = tv
                W("scalar", "te", tv)
                dst = vaug[:, tt * GRP * 65:(tt + 1) * GRP * 65] \
                    .rearrange("p (h c) -> p h c", c=65)[:, :, 0:64]
                src = psb[bi][:, :].rearrange("p (h c) -> p h c", c=64)
                sv = V("scalar", lambda e, dst=dst, src=src: e.copy(
                    dst, src), inc="sc")
                banks.release(bi, "sc", sv)

    def l0_attention():
        with sbt("ctxT", [128, ND * S], BF16) as ctxT:
            def l0_group(g, QT, KT, Vaug):
                with sbt("ropeC", [128, S], F32) as ropeC, \
                     sbt("ropeS", [128, S], F32) as ropeS:
                    col_mod(colf[:, 7:8], colf[:, 0:1], 64.0,
                            colf[:, 30:31], colf[:, 31:32])
                    V("vector", lambda e: e.tensor_single_scalar(
                        colf[:, 8:9], colf[:, 7:8], 0.5, ALU.mult),
                        inc="ve")
                    col_mod(colf[:, 9:10], colf[:, 8:9], 1.0,
                            colf[:, 30:31], colf[:, 31:32])
                    V("vector", lambda e: e.tensor_sub(
                        colf[:, 8:9], colf[:, 8:9], colf[:, 9:10]),
                        inc="ve")
                    vv = V("vector", lambda e: e.tensor_single_scalar(
                        colf[:, 9:10], colf[:, 8:9],
                        -2.0 / 64.0 * math.log(10000.0), ALU.mult),
                        inc="ve")
                    W("scalar", "ve", vv)
                    sv = V("scalar", lambda e: e.activation(
                        colf[:, 10:11], colf[:, 9:10], AF.Exp),
                        inc="sc")
                    gv = V("gpsimd", lambda e: e.iota(
                        ropeS[:, :], [[1, S]], base=0,
                        channel_multiplier=0, allow_small_or_imprecise_dtypes=True), inc="gp")
                    W("vector", "sc", sv)
                    W("vector", "gp", gv)
                    col_mod(colf[:, 11:12], colf[:, 0:1], 2.0,
                            colf[:, 30:31], colf[:, 31:32])
                    V("vector", lambda e: e.tensor_scalar(
                        colf[:, 11:12], colf[:, 11:12], 2.0, -1.0,
                        ALU.mult, ALU.add), inc="ve")
                    for c in range(NTCH):
                        V("vector", lambda e, c=c: e.tensor_scalar(
                            tmpA[:, :], ropeS[:, ts(c, 512)],
                            colf[:, 10:11], None, ALU.mult), inc="ve")
                        for dst, phase, signed in (
                                (ropeC, math.pi / 2.0, False),
                                (ropeS, 0.0, True)):
                            V("vector", lambda e, ph=phase:
                              e.tensor_scalar(
                                  tmpB[:, :], tmpA[:, :], ph,
                                  1.0 / TWO_PI, ALU.add, ALU.mult),
                              inc="ve")
                            for bit in (256.0, 128.0, 64.0, 32.0, 16.0,
                                        8.0, 4.0, 2.0, 1.0):
                                V("vector", lambda e, bit=bit:
                                  e.tensor_single_scalar(
                                      tmpC[:, :], tmpB[:, :], bit,
                                      ALU.is_ge), inc="ve")
                                V("vector", lambda e, bit=bit:
                                  e.scalar_tensor_tensor(
                                      tmpB[:, :], tmpC[:, :], -bit,
                                      tmpB[:, :], ALU.mult, ALU.add),
                                  inc="ve")
                            V("vector", lambda e:
                              e.tensor_single_scalar(
                                  tmpB[:, :], tmpB[:, :], TWO_PI,
                                  ALU.mult), inc="ve")
                            V("vector", lambda e: e.tensor_single_scalar(
                                tmpC[:, :], tmpB[:, :], math.pi,
                                ALU.is_ge), inc="ve")
                            V("vector", lambda e:
                              e.scalar_tensor_tensor(
                                  tmpB[:, :], tmpC[:, :], -TWO_PI,
                                  tmpB[:, :], ALU.mult, ALU.add),
                              inc="ve")
                            V("vector", lambda e: e.tensor_single_scalar(
                                tmpC[:, :], tmpB[:, :], -math.pi,
                                ALU.is_lt), inc="ve")
                            vv = V("vector", lambda e:
                                   e.scalar_tensor_tensor(
                                       tmpB[:, :], tmpC[:, :], TWO_PI,
                                       tmpB[:, :], ALU.mult, ALU.add),
                                   inc="ve")
                            W("scalar", "ve", vv)
                            sv = V("scalar", lambda e, dst=dst, c=c:
                                   e.activation(dst[:, ts(c, 512)],
                                                tmpB[:, :], AF.Sin),
                                   inc="sc")
                            W("vector", "sc", sv)
                            if signed:
                                V("vector", lambda e, dst=dst, c=c:
                                  e.tensor_scalar(
                                      dst[:, ts(c, 512)],
                                      dst[:, ts(c, 512)],
                                      colf[:, 11:12], None, ALU.mult),
                                  inc="ve")
                    p.barrier()
                    proj_feature(QT, "WQ0T", scr["XNT0"], g,
                                 rope=(ropeC, ropeS))
                    proj_feature(KT, "WK0T", scr["XNT0"], g,
                                 rope=(ropeC, ropeS))
                proj_token(Vaug, "WV0T", scr["XNT0"], g)
                qk_bound(QT, KT, 15, 0.125)
                with sbt("PT0", [128, NT * 512], BF16) as PT, \
                     sbt("cn0", [64, 512], BF16) as cnb:
                    prev_k1 = [0, 0]
                    for qc in range(NTCH):
                        for hl in range(GRP):
                            W("scalar", "te", p.cnt["te"])
                            W("scalar", "ve", p.cnt["ve"])
                            h = g * GRP + hl
                            pbase = (hl % 2) * 64
                            chunk = hl // 2
                            tp = (64, 0) if pbase else None
                            pt_done = 0
                            for kt in range(NT):
                                bi = banks.acquire()
                                tv = V("tensor",
                                       lambda e, kt=kt, bi=bi, qc=qc,
                                       pbase=pbase, chunk=chunk, tp=tp:
                                       nc.tensor.matmul(
                                           psb[bi][:, :],
                                           KT[pbase:pbase + 64,
                                              chunk * S + kt * 128:
                                              chunk * S + kt * 128
                                              + 128],
                                           QT[pbase:pbase + 64,
                                              chunk * S + qc * 512:
                                              chunk * S + qc * 512
                                              + 512],
                                           start=True, stop=True,
                                           tile_position=tp), inc="te")
                                W("scalar", "te", tv)
                                sv = V("scalar", lambda e, bi=bi, kt=kt:
                                       e.activation(
                                           PT[:, ts(kt, 512)],
                                           psb[bi][:, :], AF.Exp,
                                           bias=colf[:, 15:16],
                                           scale=cst[:, 1:2]), inc="sc")
                                banks.release(bi, "sc", sv)
                                pt_done = sv
                            W("tensor", "sc", pt_done)
                            bi = banks.acquire()
                            tv = None
                            for kt in range(NT):
                                last = kt == NT - 1
                                tv = V("tensor",
                                       lambda e, kt=kt, bi=bi, hl=hl,
                                       f=(kt == 0), l=last:
                                       nc.tensor.matmul(
                                           psb[bi][0:65, :],
                                           Vaug[:, kt * GRP * 65
                                                + hl * 65:
                                                kt * GRP * 65 + hl * 65
                                                + 65],
                                           PT[:, ts(kt, 512)],
                                           start=f, stop=l),
                                       inc="te" if last else None)
                            W("scalar", "te", tv)
                            W("scalar", "ve", prev_k1[1])
                            sv = V("scalar", lambda e, bi=bi: e.copy(
                                rowf[0:1, :], psb[bi][64:65, :]),
                                inc="sc")
                            W("vector", "sc", sv)
                            W("vector", "te", prev_k1[0])
                            V("vector", lambda e: e.reciprocal(
                                rowf[0:1, :], rowf[0:1, :]), inc="ve")
                            vv = V("vector", lambda e: e.tensor_copy(
                                rowb[0:1, :], rowf[0:1, :]), inc="ve")
                            W("tensor", "ve", vv)
                            bj = banks.acquire()
                            tvb = V("tensor", lambda e, bj=bj:
                                    nc.tensor.matmul(
                                        psb[bj][0:64, :],
                                        onesbf[0:1, 0:64],
                                        rowb[0:1, :], start=True,
                                        stop=True), inc="te")
                            prev_k1[0] = tvb
                            W("scalar", "te", tvb)
                            sv = V("scalar", lambda e, bj=bj: e.copy(
                                cnb[:, :], psb[bj][0:64, :]), inc="sc")
                            banks.release(bj, "sc", sv)
                            prev_k1[1] = sv
                            W("vector", "sc", sv)
                            vv = V("vector",
                                   lambda e, bi=bi, h=h, qc=qc:
                                   e.tensor_tensor(
                                       ctxT[(h % 2) * 64:
                                            (h % 2) * 64 + 64,
                                            (h // 2) * S + qc * 512:
                                            (h // 2) * S + qc * 512
                                            + 512],
                                       psb[bi][0:64, :], cnb[:, :],
                                       ALU.mult), inc="ve")
                            banks.release(bi, "ve", vv)
            for g in range(NGRP):
                with sbt("QT0", [128, ECH_G * S], BF16) as QT, \
                     sbt("KT0", [128, ECH_G * S], BF16) as KT, \
                     sbt("Vaug0", [128, NT * GRP * 65],
                                    BF16) as Vaug:
                    l0_group(g, QT, KT, Vaug)
                p.barrier()
            with sbt("wo0", [128, ND * D], BF16) as wo:
                dv = 0
                for dc in range(ND):
                    dv = dma("gpsimd", wo[:, ts(dc, D)],
                             scr["WO0T"][ts(dc, 128), :])
                W("tensor", "d_g", dv)
                for tt in range(NT):
                    for ec2 in range(2):
                        bi = banks.acquire()
                        tv = None
                        for dc in range(ND):
                            last = dc == ND - 1
                            tv = V("tensor",
                                   lambda e, dc=dc, bi=bi, tt=tt, ec2=ec2,
                                   f=(dc == 0), l=last: nc.tensor.matmul(
                                       psb[bi][:, :],
                                       ctxT[:, dc * S + tt * 128:
                                            dc * S + tt * 128 + 128],
                                       wo[:, dc * D + ec2 * 512:
                                          dc * D + ec2 * 512 + 512],
                                       start=f, stop=l),
                                   inc="te" if last else None)
                        W("vector", "te", tv)
                        rsl = res[:, tt * D + ec2 * 512:
                                  tt * D + ec2 * 512 + 512]
                        vv = V("vector", lambda e, bi=bi, rsl=rsl:
                               e.tensor_add(rsl, rsl, psb[bi][:, :]),
                               inc="ve")
                        banks.release(bi, "ve", vv)
        p.barrier()

    def l0_moe():
        with sbt("xnT0", [128, ND * S], BF16) as xnT, \
             sbt("W8", [128, NT * NE], F32) as W8:
            for dc in range(ND):
                dma("sync", xnT[:, ts(dc, S)], scr["XNT0"][ts(dc, 128), :])
            p.barrier()
            with sbt("rw", [128, ND * 8], BF16) as rw, \
                 sbt("r8a", [128, 8], F32) as r8a, \
                 sbt("r8b", [128, 8], F32) as r8b, \
                 sbt("r8c", [128, 8], F32) as r8c, \
                 sbt("io8", [128, 8], F32) as io8, \
                 sbt("bg8", [128, 8], F32) as bg8:
                dv = 0
                for dc in range(ND):
                    dv = dma("gpsimd", rw[:, ts(dc, 8)],
                             w["L0_ffn_router_w"][ts(dc, 128), :])
                gv = V("gpsimd", lambda e: e.iota(
                    io8[:, :], [[1, 8]], base=0, channel_multiplier=0, allow_small_or_imprecise_dtypes=True),
                    inc="gp")
                W("vector", "gp", gv)
                V("vector", lambda e: e.tensor_single_scalar(
                    bg8[:, :], io8[:, :], -99.0, ALU.add), inc="ve")
                W("tensor", "d_g", dv)
                W("vector", "gp", gv)
                for tt in range(NT):
                    W("vector", "sc", p.cnt["sc"])
                    bi = banks.acquire()
                    tv = None
                    for dc in range(ND):
                        last = dc == ND - 1
                        tv = V("tensor", lambda e, dc=dc, bi=bi, tt=tt,
                               f=(dc == 0), l=last: nc.tensor.matmul(
                                   psb[bi][:, 0:8],
                                   xnT[:, dc * S + tt * 128:
                                       dc * S + tt * 128 + 128],
                                   rw[:, ts(dc, 8)], start=f, stop=l),
                               inc="te" if last else None)
                    W("vector", "te", tv)
                    W("scalar", "te", tv)
                    V("scalar", lambda e, bi=bi: e.copy(
                        junkB[:, 0:8], psb[bi][:, 0:8]), inc="sc")
                    V("vector", lambda e, bi=bi: e.tensor_copy(
                        r8a[:, :], psb[bi][:, 0:8]), inc="ve")
                    VSP(); VSP(); VSP(); VSP()
                    vv = V("vector", lambda e: e.tensor_reduce(
                        colf[:, 16:17], r8a[:, :], AX.X, ALU.max), inc="ve")
                    banks.release(bi, "ve", vv)
                    VSP(); VSP(); VSP()
                    V("vector", lambda e: e.tensor_scalar(
                        r8b[:, :], r8a[:, :], colf[:, 16:17], None,
                        ALU.is_equal), inc="ve")
                    VSP(); VSP()
                    V("vector", lambda e: e.tensor_tensor(
                        r8c[:, :], r8b[:, :], bg8[:, :], ALU.mult), inc="ve")
                    VSP(); VSP()
                    V("vector", lambda e: e.tensor_single_scalar(
                        r8c[:, :], r8c[:, :], 99.0, ALU.add), inc="ve")
                    VSP(); VSP()
                    V("vector", lambda e: e.tensor_reduce(
                        colf[:, 17:18], r8c[:, :], AX.X, ALU.min), inc="ve")
                    VSP(); VSP(); VSP()
                    V("vector", lambda e: e.tensor_scalar(
                        r8b[:, :], io8[:, :], colf[:, 17:18], None,
                        ALU.is_equal), inc="ve")
                    VSP(); VSP()
                    V("vector", lambda e: e.scalar_tensor_tensor(
                        r8c[:, :], r8b[:, :], -1e30, r8a[:, :],
                        ALU.mult, ALU.add), inc="ve")
                    VSP(); VSP()
                    V("vector", lambda e: e.tensor_reduce(
                        colf[:, 18:19], r8c[:, :], AX.X, ALU.max), inc="ve")
                    VSP(); VSP(); VSP()
                    V("vector", lambda e: e.tensor_scalar(
                        r8a[:, :], r8c[:, :], colf[:, 18:19], None,
                        ALU.is_equal), inc="ve")
                    VSP(); VSP()
                    V("vector", lambda e: e.tensor_tensor(
                        r8c[:, :], r8a[:, :], bg8[:, :], ALU.mult), inc="ve")
                    VSP(); VSP()
                    V("vector", lambda e: e.tensor_single_scalar(
                        r8c[:, :], r8c[:, :], 99.0, ALU.add), inc="ve")
                    VSP(); VSP()
                    V("vector", lambda e: e.tensor_reduce(
                        colf[:, 19:20], r8c[:, :], AX.X, ALU.min), inc="ve")
                    VSP(); VSP(); VSP()
                    V("vector", lambda e: e.tensor_scalar(
                        r8c[:, :], io8[:, :], colf[:, 19:20], None,
                        ALU.is_equal), inc="ve")
                    VSP(); VSP()
                    vv = V("vector", lambda e: e.tensor_sub(
                        colf[:, 20:21], colf[:, 16:17], colf[:, 18:19]),
                        inc="ve")
                    W("scalar", "ve", vv)
                    sv = V("scalar", lambda e: e.activation(
                        colf[:, 21:22], colf[:, 20:21], AF.Sigmoid),
                        inc="sc")
                    W("vector", "sc", sv)
                    V("vector", lambda e: e.tensor_scalar(
                        colf[:, 22:23], colf[:, 21:22], -1.0, 1.0,
                        ALU.mult, ALU.add), inc="ve")
                    V("vector", lambda e: e.tensor_scalar(
                        r8b[:, :], r8b[:, :], colf[:, 21:22], None,
                        ALU.mult), inc="ve")
                    V("vector", lambda e, tt=tt: e.scalar_tensor_tensor(
                        W8[:, ts(tt, 8)], r8c[:, :], colf[:, 22:23],
                        r8b[:, :], ALU.mult, ALU.add), inc="ve")
            p.barrier()
            NTP = MOE_TP // 512
            NTT = MOE_TP // 128
            with sbt("acts", [128, NFG * MOE_TP], BF16) as acts, \
                 sbt("gub", [128, 2 * ND * 256], BF16) as gub, \
                 sbt("dsl", [128, NFG * 512], BF16) as dsl, \
                 sbt("sil", [128, 512], F32) as sil:

                def load_gu(eidx, fg, buf):
                    dv = 0
                    base = buf * ND * 256
                    for mat, src in ((0, w["L0_ffn_gate_proj"]),
                                     (1, w["L0_ffn_up_proj"])):
                        for dc in range(ND):
                            dv = dma("gpsimd",
                                     gub[:, base + dc * 256 + mat * 128:
                                         base + dc * 256 + mat * 128 + 128],
                                     src[eidx, ts(dc, 128),
                                         fg * 128:fg * 128 + 128])
                    return dv

                gu_tok = {}
                seq = [(tp, e2, fg) for tp in range(N_MOE_TP)
                       for e2 in range(NE) for fg in range(NFG)]
                gu_tok[0] = load_gu(0, 0, 0)
                down_done = 0
                for si, (tp, eidx, fg) in enumerate(seq):
                    buf = si % 2
                    if si + 1 < len(seq):
                        ntp, ne2, nfg = seq[si + 1]
                        W("gpsimd", "te", p.cnt["te"])
                        gu_tok[1 - buf] = load_gu(ne2, nfg, 1 - buf)
                    W("tensor", "d_g", gu_tok[buf])
                    base = buf * ND * 256
                    for tc in range(NTP):
                        big = banks.acquire()
                        biu = banks.acquire()
                        for mat, bsel in ((0, big), (1, biu)):
                            tv = None
                            for dc in range(ND):
                                last = dc == ND - 1
                                tv = V("tensor",
                                       lambda e, dc=dc, bi=bsel, mat=mat,
                                       tc=tc, tp=tp, base=base, f=(dc == 0),
                                       l=last: nc.tensor.matmul(
                                           psb[bi][:, :],
                                           gub[:, base + dc * 256
                                               + mat * 128:
                                               base + dc * 256 + mat * 128
                                               + 128],
                                           xnT[:, dc * S + tp * MOE_TP
                                               + tc * 512:
                                               dc * S + tp * MOE_TP
                                               + tc * 512 + 512],
                                           start=f, stop=l),
                                       inc="te" if last else None)
                        W("scalar", "te", tv)
                        W("scalar", "ve", p.cnt["ve"])
                        sv = V("scalar", lambda e, bi=big: e.activation(
                            sil[:, :], psb[bi][:, :], AF.Silu), inc="sc")
                        banks.release(big, "sc", sv)
                        W("vector", "sc", sv)
                        W("vector", "te", down_done)
                        vv = V("vector", lambda e, bi=biu, fg=fg, tc=tc:
                               e.tensor_tensor(
                                   acts[:, fg * MOE_TP + tc * 512:
                                        fg * MOE_TP + tc * 512 + 512],
                                   sil[:, :], psb[bi][:, :], ALU.mult),
                               inc="ve")
                        banks.release(biu, "ve", vv)
                    if fg == NFG - 1:
                        W("tensor", "ve", p.cnt["ve"])
                        for ec2 in range(2):
                            W("gpsimd", "te", p.cnt["te"])
                            dtok = 0
                            for fgd in range(NFG):
                                dtok = dma(
                                    "gpsimd", dsl[:, ts(fgd, 512)],
                                    w["L0_ffn_down_proj"][eidx,
                                                          ts(fgd, 128),
                                                          ts(ec2, 512)])
                            W("tensor", "d_g", dtok)
                            for tlh in range(NTT // 4):
                                dbanks = [banks.acquire() for _ in range(4)]
                                tvs = [None] * 4
                                for fgd in range(NFG):
                                    for tl4 in range(4):
                                        tl = tlh * 4 + tl4
                                        last = fgd == NFG - 1
                                        tvs[tl4] = V(
                                            "tensor",
                                            lambda e, fgd=fgd, tl=tl,
                                            tl4=tl4, bi=dbanks[tl4],
                                            f=(fgd == 0), l=last:
                                            nc.tensor.matmul(
                                                psb[bi][:, :],
                                                acts[:, fgd * MOE_TP
                                                     + tl * 128:
                                                     fgd * MOE_TP
                                                     + tl * 128 + 128],
                                                dsl[:, ts(fgd, 512)],
                                                start=f, stop=l),
                                            inc="te" if last else None)
                                for tl4 in range(4):
                                    W("vector", "te", tvs[tl4])
                                    VSP()
                                    VSP()
                                    VSP()
                                    ttg = tp * NTT + tlh * 4 + tl4
                                    rsl = res[:, ttg * D + ec2 * 512:
                                              ttg * D + ec2 * 512 + 512]
                                    vv = V("vector",
                                           lambda e, bi=dbanks[tl4],
                                           rsl=rsl, ttg=ttg, eidx=eidx:
                                           e.scalar_tensor_tensor(
                                               rsl, psb[bi][:, :],
                                               W8[:, ttg * 8 + eidx:
                                                  ttg * 8 + eidx + 1],
                                               rsl, ALU.mult, ALU.add),
                                           inc="ve")
                                    banks.release(dbanks[tl4], "ve", vv)
                        down_done = p.cnt["te"]
        p.barrier()

    def l1_attention():
        with sbt("iotaQ", [128, 128], F32) as iotaQ, \
             sbt("sidb", [128, S], FP16) as sidb, \
             sbt("vmb", [1, 1024], BF16) as vmb:
            gv = V("gpsimd", lambda e: e.iota(
                iotaQ[:, :], [[1, 128]], base=0, channel_multiplier=128,
                allow_small_or_imprecise_dtypes=True), inc="gp")
            W("vector", "gp", gv)
            col_mod(colf[:, 24:25], colf[:, 0:1], 16.0,
                    colf[:, 30:31], colf[:, 31:32])
            V("vector", lambda e: e.tensor_sub(
                colf[:, 25:26], colf[:, 0:1], colf[:, 24:25]), inc="ve")
            V("vector", lambda e: e.tensor_single_scalar(
                colf[:, 25:26], colf[:, 25:26], 128.0, ALU.mult), inc="ve")
            V("vector", lambda e: e.tensor_scalar(
                iotaQ[:, :], iotaQ[:, :], colf[:, 25:26], None,
                ALU.subtract), inc="ve")
            with sbt("sid_i", [1, S], I32) as sid_i, \
                 sbt("sid_f", [1, S], F32) as sid_f:
                dv = dma("sync", sid_i[:, :],
                         w["L1_attn_sample_idx"][None, :])
                W("vector", "d_s", dv)
                V("vector", lambda e: e.tensor_copy(
                    sid_f[:, :], sid_i[:, :]), inc="ve")
                bcast_row(sidb[:, :], sid_f[0:1, :], S)
            p.barrier()

            def l1_group(g, QT, KT, Vaug, Qtok, Mcols, Msum):
                proj_feature(QT, "WQ1T", scr["XNT1"], g,
                             bias_src=w["L1_attn_bq"])
                proj_feature(KT, "WK1T", scr["XNT1"], g,
                             bias_src=w["L1_attn_bk"])
                proj_token(Vaug, "WV1T", scr["XNT1"], g,
                           bias_src=w["L1_attn_bv"])
                with sbt("Ktok", [128, NT * 512],
                                    BF16) as Ktok, \
                     sbt("KS", [128, ECH_G * S], BF16) as KS, \
                     sbt("Sjc", [128, NT * 256], BF16) as Sjc:
                    for srcT, dstTok in ((QT, Qtok), (KT, Ktok)):
                        for tt in range(NT):
                            dst = dstTok[:, tt * 512:tt * 512 + 512] \
                                .rearrange("p (ec c) -> p ec c", c=128)
                            sv = transpose_batch(
                                [srcT[:, ec * S + tt * 128:
                                      ec * S + tt * 128 + 128]
                                 for ec in range(ECH_G)],
                                dstTok[:, tt * 512:tt * 512 + 512])
                    for jc in range(S // 256):
                        W("vector", "te", p.cnt["te"])
                        vv = None
                        for tt in range(NT):
                            vv = V("vector", lambda e, tt=tt, jc=jc:
                                   e.tensor_scalar(
                                       Sjc[:, ts(tt, 256)],
                                       sidb[:, ts(jc, 256)],
                                       iotaT16[:, tt:tt + 1], None,
                                       ALU.is_equal), inc="ve")
                        W("tensor", "ve", vv)
                        for pc in range(ECH_G):
                            bi = banks.acquire()
                            tv = None
                            for tt in range(NT):
                                last = tt == NT - 1
                                tv = V("tensor",
                                       lambda e, tt=tt, bi=bi, pc=pc,
                                       f=(tt == 0), l=last:
                                       nc.tensor.matmul(
                                           psb[bi][:, 0:256],
                                           Ktok[:, tt * 512 + pc * 128:
                                                tt * 512 + pc * 128
                                                + 128],
                                           Sjc[:, ts(tt, 256)],
                                           start=f, stop=l),
                                       inc="te" if last else None)
                            W("scalar", "te", tv)
                            sv = V("scalar",
                                   lambda e, bi=bi, pc=pc, jc=jc:
                                   e.copy(KS[:, pc * S + jc * 256:
                                             pc * S + jc * 256 + 256],
                                          psb[bi][:, 0:256]), inc="sc")
                            banks.release(bi, "sc", sv)
                    with sbt("ksc", [128, ECH_G],
                                        BF16) as ksc, \
                         sbt("kscf", [128, ECH_G],
                                        F32) as kscf:
                        W("vector", "sc", p.cnt["sc"])
                        for pc in range(ECH_G):
                            V("vector", lambda e, pc=pc: e.tensor_reduce(
                                kscf[:, pc:pc + 1], KS[:, ts(pc, S)],
                                AX.X, ALU.add), inc="ve")
                        vv = V("vector", lambda e: e.tensor_copy(
                            ksc[:, :], kscf[:, :]), inc="ve")
                        W("tensor", "ve", vv)
                        for hl in range(GRP):
                            pbase = (hl % 2) * 64
                            chunk = hl // 2
                            tp2 = (64, 0) if pbase else None
                            bi = banks.acquire()
                            tv = None
                            for qt in range(NT):
                                tv = V("tensor",
                                       lambda e, qt=qt, bi=bi,
                                       pbase=pbase, chunk=chunk,
                                       tp2=tp2: nc.tensor.matmul(
                                           psb[bi][:, qt:qt + 1],
                                           QT[pbase:pbase + 64,
                                              chunk * S + qt * 128:
                                              chunk * S + qt * 128
                                              + 128],
                                           ksc[pbase:pbase + 64,
                                               chunk:chunk + 1],
                                           start=True, stop=True,
                                           tile_position=tp2),
                                       inc="te")
                            W("vector", "te", tv)
                            vv = V("vector", lambda e, bi=bi, hl=hl:
                                   e.tensor_copy(Msum[:, ts(hl, NT)],
                                                 psb[bi][:, 0:NT]),
                                   inc="ve")
                            banks.release(bi, "ve", vv)
                    vv = V("vector", lambda e: e.memset(
                        Mcols[:, :], -1e30), inc="ve")
                    for hl in range(GRP):
                        pbase = (hl % 2) * 64
                        chunk = hl // 2
                        tp2 = (64, 0) if pbase else None
                        for qt in range(NT):
                            for jc in range(NTCH):
                                bi = banks.acquire()
                                tv = V("tensor",
                                       lambda e, bi=bi, qt=qt, jc=jc,
                                       pbase=pbase, chunk=chunk,
                                       tp2=tp2: nc.tensor.matmul(
                                           psb[bi][:, :],
                                           QT[pbase:pbase + 64,
                                              chunk * S + qt * 128:
                                              chunk * S + qt * 128
                                              + 128],
                                           KS[pbase:pbase + 64,
                                              chunk * S + jc * 512:
                                              chunk * S + jc * 512
                                              + 512],
                                           start=True, stop=True,
                                           tile_position=tp2),
                                       inc="te")
                                W("vector", "te", tv)
                                V("vector", lambda e, bi=bi:
                                  e.tensor_reduce(
                                      colf[:, 26:27], psb[bi][:, :],
                                      AX.X, ALU.max), inc="ve")
                                col = hl * NT + qt
                                vv = V("vector", lambda e, col=col:
                                       e.tensor_max(
                                           Mcols[:, col:col + 1],
                                           Mcols[:, col:col + 1],
                                           colf[:, 26:27]), inc="ve")
                                banks.release(bi, "ve", vv)
                vv = V("vector", lambda e: e.scalar_tensor_tensor(
                    Mcols[:, :], Msum[:, :], -1.0 / S, Mcols[:, :],
                    ALU.mult, ALU.add), inc="ve")
                with ExitStack() as exs:
                    en = exs.enter_context
                    M8 = en(sbt("M8", [128, 128], F32))
                    idxm = en(sbt("idxm", [128, U1], F32))
                    t32a = en(sbt("t32a", [128, 1], F32))
                    t32b = en(sbt("t32b", [128, 1], F32))
                    e128 = en(sbt("e128", [128, 128], F32))
                    bg128 = en(sbt("bg1", [128, 128], F32))
                    idxr = en(sbt("idxr", [1, GRP * U1], F32))
                    idxb = en(sbt("idxb", [128, GRP * U1],
                                             F32))
                    idxmT = en(sbt("idxmT", [U1, 128], F32))
                    OHT = en(sbt("OHT", [128, NT * GRP * U1],
                                            BF16))
                    Qr = en(sbt("Qr", [128, ECH_G * U1],
                                           BF16))
                    PT1 = en(sbt("PT1", [128, NT * GRP * U1],
                                            BF16))
                    ctxn = en(sbt("ctxn", [U1, GRP * 65],
                                             BF16))
                    iotaTt = en(sbt("iotaTt", [U1, S], FP16))
                    W("gpsimd", "te", p.cnt["te"])
                    W("gpsimd", "ve", p.cnt["ve"])
                    W("gpsimd", "sc", p.cnt["sc"])
                    gvi = V("gpsimd", lambda e: e.iota(
                        iotaTt[:, :], [[1, S]], base=0,
                        channel_multiplier=0,
                        allow_small_or_imprecise_dtypes=True), inc="gp")
                    W("vector", "gp", gvi)
                    W("tensor", "ve", vv)
                    bi_t = banks.acquire()
                    tvt = V("tensor", lambda e, bi=bi_t:
                            nc.tensor.transpose(
                                psb[bi][:, 0:128], Mcols[:, :],
                                eyef[:, :]), inc="te")
                    W("vector", "te", tvt)
                    vvt = V("vector", lambda e, bi=bi_t: e.tensor_copy(
                        M8[:, :], psb[bi][:, 0:128]), inc="ve")
                    banks.release(bi_t, "ve", vvt)
                    VSP(); VSP(); VSP(); VSP()
                    V("vector", lambda e: e.tensor_single_scalar(
                        bg128[:, :], iotaQ[:, :], -1e6, ALU.add),
                      inc="ve")
                    qk_bound(QT, KT, 27, 0.125)
                    for it in range(U1):
                        V("vector", lambda e: e.tensor_reduce(
                            t32a[:, :], M8[:, :], AX.X, ALU.max),
                            inc="ve")
                        VSP(); VSP()
                        for k in (8, 4, 2, 1):
                            V("vector", lambda e, k=k: e.stream_shuffle(
                                t32b[:, :], t32a[:, :],
                                [i ^ k for i in range(32)]), inc="ve")
                            VSP(); VSP()
                            V("vector", lambda e: e.tensor_max(
                                t32a[:, :], t32a[:, :], t32b[:, :]),
                                inc="ve")
                            VSP(); VSP()
                        V("vector", lambda e: e.tensor_scalar(
                            e128[:, :], M8[:, :], t32a[:, :], None,
                            ALU.is_equal), inc="ve")
                        VSP(); VSP()
                        V("vector", lambda e: e.tensor_tensor(
                            e128[:, :], e128[:, :], bg128[:, :],
                            ALU.mult), inc="ve")
                        VSP(); VSP()
                        V("vector", lambda e: e.tensor_single_scalar(
                            e128[:, :], e128[:, :], 1e6, ALU.add),
                            inc="ve")
                        VSP(); VSP()
                        V("vector", lambda e: e.tensor_reduce(
                            t32a[:, :], e128[:, :], AX.X, ALU.min),
                            inc="ve")
                        VSP(); VSP()
                        for k in (8, 4, 2, 1):
                            V("vector", lambda e, k=k: e.stream_shuffle(
                                t32b[:, :], t32a[:, :],
                                [i ^ k for i in range(32)]), inc="ve")
                            VSP(); VSP()
                            V("vector", lambda e: e.tensor_tensor(
                                t32a[:, :], t32a[:, :], t32b[:, :],
                                ALU.min), inc="ve")
                            VSP(); VSP()
                        V("vector", lambda e, it=it: e.tensor_copy(
                            idxm[:, it:it + 1], t32a[:, :]), inc="ve")
                        V("vector", lambda e: e.tensor_scalar(
                            e128[:, :], iotaQ[:, :], t32a[:, :], None,
                            ALU.is_equal), inc="ve")
                        VSP(); VSP()
                        V("vector", lambda e: e.scalar_tensor_tensor(
                            M8[:, :], e128[:, :], -1e30, M8[:, :],
                            ALU.mult, ALU.add), inc="ve")
                        VSP(); VSP(); VSP()
                    vv = p.cnt["ve"]
                    W("sync", "ve", vv)
                    for hl in range(GRP):
                        dma("sync", idxr[0:1, ts(hl, U1)],
                            idxm[hl * 16:hl * 16 + 1, :])
                    W("tensor", "ve", p.cnt["ve"])
                    bi_t = banks.acquire()
                    tvt = V("tensor", lambda e, bi=bi_t:
                            nc.tensor.transpose(
                                psb[bi][0:U1, 0:128], idxm[:, :],
                                eyef[:, :]), inc="te")
                    W("vector", "te", tvt)
                    vvt = V("vector", lambda e, bi=bi_t: e.tensor_copy(
                        idxmT[:, :], psb[bi][0:U1, 0:128]), inc="ve")
                    banks.release(bi_t, "ve", vvt)
                    W("vector", "d_s", p.cnt["d_s"])
                    bcast_row(idxb[:, :], idxr[0:1, :], GRP * U1)
                    W("vector", "ve", vvt)
                    oht_done = None
                    for tt in range(NT):
                        oht_done = V("vector", lambda e, tt=tt:
                                     e.tensor_scalar(
                                         OHT[:, ts(tt, GRP * U1)],
                                         idxb[:, :],
                                         iotaT16[:, tt:tt + 1], None,
                                         ALU.is_equal), inc="ve")
                    W("tensor", "ve", oht_done)
                    for hl in range(GRP):
                        bi = banks.acquire()
                        tv = None
                        for tt in range(NT):
                            last = tt == NT - 1
                            tv = V("tensor",
                                   lambda e, tt=tt, bi=bi, hl=hl,
                                   f=(tt == 0), l=last:
                                   nc.tensor.matmul(
                                       psb[bi][0:64, 0:U1],
                                       Qtok[:, tt * 512
                                            + (hl // 2) * 128
                                            + (hl % 2) * 64:
                                            tt * 512 + (hl // 2) * 128
                                            + (hl % 2) * 64 + 64],
                                       OHT[:, tt * GRP * U1 + hl * U1:
                                           tt * GRP * U1 + hl * U1
                                           + U1],
                                       start=f, stop=l),
                                   inc="te" if last else None)
                        W("scalar", "te", tv)
                        sv = V("scalar", lambda e, bi=bi, hl=hl: e.copy(
                            Qr[(hl % 2) * 64:(hl % 2) * 64 + 64,
                               (hl // 2) * U1:(hl // 2) * U1 + U1],
                            psb[bi][0:64, 0:U1]), inc="sc")
                        banks.release(bi, "sc", sv)
                    W("tensor", "sc", p.cnt["sc"])
                    for hl in range(GRP):
                        pbase = (hl % 2) * 64
                        chunk = hl // 2
                        tp2 = (64, 0) if pbase else None
                        for tt in range(NT):
                            bi = banks.acquire()
                            tv = V("tensor",
                                   lambda e, bi=bi, tt=tt, pbase=pbase,
                                   chunk=chunk, tp2=tp2:
                                   nc.tensor.matmul(
                                       psb[bi][:, 0:U1],
                                       KT[pbase:pbase + 64,
                                          chunk * S + tt * 128:
                                          chunk * S + tt * 128 + 128],
                                       Qr[pbase:pbase + 64,
                                          chunk * U1:chunk * U1 + U1],
                                       start=True, stop=True,
                                       tile_position=tp2), inc="te")
                            W("scalar", "te", tv)
                            sv = V("scalar",
                                   lambda e, bi=bi, tt=tt, hl=hl:
                                   e.activation(
                                       PT1[:, tt * GRP * U1 + hl * U1:
                                           tt * GRP * U1 + hl * U1
                                           + U1],
                                       psb[bi][:, 0:U1], AF.Exp,
                                       bias=colf[:, 27:28],
                                       scale=cst[:, 1:2]), inc="sc")
                            banks.release(bi, "sc", sv)
                    V("vector", lambda e: e.memset(
                        ctxn[:, :].rearrange("p (h c) -> p h c", c=65)
                        [:, :, 64:65], 1.0), inc="ve")
                    W("scalar", "ve", p.cnt["ve"])
                    W("tensor", "sc", p.cnt["sc"])
                    for hl in range(GRP):
                        bi = banks.acquire()
                        tv = None
                        for tt in range(NT):
                            last = tt == NT - 1
                            tv = V("tensor",
                                   lambda e, tt=tt, bi=bi, hl=hl,
                                   f=(tt == 0), l=last:
                                   nc.tensor.matmul(
                                       psb[bi][0:U1, 0:65],
                                       PT1[:, tt * GRP * U1 + hl * U1:
                                           tt * GRP * U1 + hl * U1
                                           + U1],
                                       Vaug[:, tt * GRP * 65 + hl * 65:
                                            tt * GRP * 65 + hl * 65
                                            + 65],
                                       start=f, stop=l),
                                   inc="te" if last else None)
                        W("vector", "te", tv)
                        vv = V("vector", lambda e, bi=bi: e.reciprocal(
                            colf[0:U1, 28:29], psb[bi][0:U1, 64:65]),
                            inc="ve")
                        W("scalar", "ve", vv)
                        sv = V("scalar", lambda e, bi=bi, hl=hl:
                               e.activation(
                                   ctxn[0:U1, hl * 65:hl * 65 + 64],
                                   psb[bi][0:U1, 0:64], AF.Copy,
                                   scale=colf[0:U1, 28:29]), inc="sc")
                        banks.release(bi, "sc", sv)
                    bi = banks.acquire()
                    tv = None
                    for hl in range(GRP):
                        for tt in range(NT):
                            last = tt == NT - 1
                            tv = V("tensor",
                                   lambda e, tt=tt, hl=hl, bi=bi,
                                   f=(tt == 0), l=last:
                                   nc.tensor.matmul(
                                       psb[bi][0:1,
                                               hl * 64:hl * 64 + 64],
                                       onesbf[:, 0:1],
                                       Vaug[:, tt * GRP * 65 + hl * 65:
                                            tt * GRP * 65 + hl * 65
                                            + 64],
                                       start=f, stop=l),
                                   inc="te" if last else None)
                    W("vector", "te", tv)
                    vv = V("vector", lambda e, bi=bi, g=g:
                           e.tensor_single_scalar(
                               vmb[0:1, ts(g, 512)], psb[bi][0:1, :],
                               1.0 / S, ALU.mult), inc="ve")
                    banks.release(bi, "ve", vv)
                    W("tensor", "sc", p.cnt["sc"])
                    with sbt("OHh", [U1, S], BF16) as OHh, \
                         sbt("osg", [64, 512], BF16) as osg:
                        ostore = [0]
                        prev = [0, 0]
                        for hl in range(GRP):
                            h = g * GRP + hl
                            W("vector", "te", prev[0])
                            vv = V("vector", lambda e, hl=hl:
                                   e.tensor_scalar(
                                       OHh[:, :], iotaTt[:, :],
                                       idxmT[:, hl * 16:hl * 16 + 1],
                                       None, ALU.is_equal), inc="ve")
                            W("tensor", "ve", vv)
                            for c4 in range(NTCH):
                                bi = banks.acquire()
                                bj = banks.acquire()
                                V("tensor",
                                  lambda e, bi=bi, hl=hl, c4=c4:
                                  nc.tensor.matmul(
                                      psb[bi][0:64, :],
                                      ctxn[0:U1,
                                           hl * 65:hl * 65 + 64],
                                      OHh[:, ts(c4, 512)],
                                      start=True, stop=False))
                                tvc = V("tensor",
                                        lambda e, bj=bj, c4=c4:
                                        nc.tensor.matmul(
                                            psb[bj][0:1, :],
                                            onesbf[0:U1, 0:1],
                                            OHh[:, ts(c4, 512)],
                                            start=True, stop=True),
                                        inc="te")
                                W("scalar", "te", tvc)
                                sv = V("scalar", lambda e, bj=bj:
                                       e.copy(rowf[0:1, :],
                                              psb[bj][0:1, :]),
                                       inc="sc")
                                banks.release(bj, "sc", sv)
                                W("vector", "sc", sv)
                                W("vector", "te", prev[0])
                                vv = V("vector", lambda e:
                                       e.tensor_scalar(
                                           rowb[0:1, :], rowf[0:1, :],
                                           -1.0, 1.0, ALU.mult,
                                           ALU.add), inc="ve")
                                W("tensor", "ve", vv)
                                tv = V("tensor",
                                       lambda e, bi=bi, g=g, hl=hl:
                                       nc.tensor.matmul(
                                           psb[bi][0:64, :],
                                           vmb[0:1, g * 512 + hl * 64:
                                               g * 512 + hl * 64 + 64],
                                           rowb[0:1, :],
                                           start=False, stop=True),
                                       inc="te")
                                prev[0] = tv
                                W("scalar", "te", tv)
                                W("scalar", "d_s", ostore[0])
                                sv = V("scalar", lambda e, bi=bi:
                                       e.copy(osg[:, :],
                                              psb[bi][0:64, :]),
                                       inc="sc")
                                banks.release(bi, "sc", sv)
                                W("sync", "sc", sv)
                                ostore[0] = dma(
                                    "sync",
                                    scr["OUTT1"][
                                        (h // 2) * 128
                                        + (h % 2) * 64:
                                        (h // 2) * 128
                                        + (h % 2) * 64 + 64,
                                        c4 * 512:c4 * 512 + 512],
                                    osg[:, :])
            for g in range(NGRP):
                with sbt("QT1", [128, ECH_G * S], BF16) as QT, \
                     sbt("KT1", [128, ECH_G * S], BF16) as KT, \
                     sbt("Vaug1", [128, NT * GRP * 65],
                                    BF16) as Vaug, \
                     sbt("Qtok", [128, NT * 512], BF16) as Qtok, \
                     sbt("Mcols", [128, 128], F32) as Mcols, \
                     sbt("Msum", [128, 128], F32) as Msum:
                    l1_group(g, QT, KT, Vaug, Qtok, Mcols, Msum)
                p.barrier()
            with sbt("wo1", [128, ND * D], BF16) as wo, \
                 sbt("oxr", [128, 2 * ND * 128], BF16) as oxr, \
                 sbt("bor", [1, D], BF16) as bor:
                dv = dma("gpsimd", bor[:, :], w["L1_attn_bo"][None, :])
                for dc in range(ND):
                    dv = dma("gpsimd", wo[:, ts(dc, D)],
                             scr["WO1T"][ts(dc, 128), :])
                W("tensor", "d_g", dv)
                ox_tok = [0, 0]
                for tt in range(NT):
                    xb = tt % 2
                    W("sync", "te", ox_tok[xb])
                    dvx = 0
                    for dc in range(ND):
                        dvx = dma("sync",
                                  oxr[:, xb * ND * 128 + dc * 128:
                                      xb * ND * 128 + dc * 128 + 128],
                                  scr["OUTT1"][ts(dc, 128),
                                               tt * 128:tt * 128 + 128])
                    W("tensor", "d_s", dvx)
                    for ec2 in range(2):
                        bi = banks.acquire()
                        for dc in range(ND):
                            V("tensor",
                              lambda e, dc=dc, bi=bi, xb=xb, ec2=ec2,
                              f=(dc == 0): nc.tensor.matmul(
                                  psb[bi][:, :],
                                  oxr[:, xb * ND * 128 + dc * 128:
                                      xb * ND * 128 + dc * 128 + 128],
                                  wo[:, dc * D + ec2 * 512:
                                     dc * D + ec2 * 512 + 512],
                                  start=f, stop=False))
                        tv = V("tensor", lambda e, bi=bi, ec2=ec2:
                               nc.tensor.matmul(
                                   psb[bi][:, :], onesbf[0:1, 0:128],
                                   bor[0:1, ec2 * 512:ec2 * 512 + 512],
                                   start=False, stop=True), inc="te")
                        ox_tok[xb] = tv
                        W("vector", "te", tv)
                        rsl = res[:, tt * D + ec2 * 512:
                                  tt * D + ec2 * 512 + 512]
                        vv = V("vector", lambda e, bi=bi, rsl=rsl:
                               e.tensor_add(rsl, rsl, psb[bi][:, :]),
                               inc="ve")
                        banks.release(bi, "ve", vv)
        p.barrier()

    def l1_ffn():
        with sbt("convc", [128, NCH * 4], F32) as convc, \
             sbt("xqp", [128, ND * (FFN_TP + 2)], BF16) as xqp, \
             sbt("w1s", [128, 2 * ND * 128], BF16) as w1s, \
             sbt("w2s", [128, 2 * D], BF16) as w2s, \
             sbt("hsb", [128, FFN_TP + 2], F32) as hsb, \
             sbt("hacc", [128, FFN_TP], F32) as hacc, \
             sbt("hact", [128, NCH * FFN_TP], F32) as hact, \
             sbt("hactb", [128, FFN_TP], BF16) as hactb, \
             sbt("hq", [128, NCH * FFN_TP], BF16) as hq, \
             sbt("mx", [128, 8], F32) as mx, \
             sbt("s2r", [1, FFN_TP], F32) as s2r, \
             sbt("s2b", [128, FFN_TP], F32) as s2b:
            for cch in range(NCH):
                dma("sync", convc[:, cch * 4:cch * 4 + 3],
                    w["L1_ffn_conv_w"][ts(cch, 128), 0, :])
                dma("sync", convc[:, cch * 4 + 3:cch * 4 + 4],
                    w["L1_ffn_conv_b"][ts(cch, 128)])
            p.barrier()
            for tp in range(N_FFN_TP):
                W("sync", "te", p.cnt["te"])
                W("vector", "te", p.cnt["te"])
                T0 = tp * FFN_TP
                dvx = 0
                for dc in range(ND):
                    dvx = dma("sync", xqp[:, ts(dc, FFN_TP + 2)],
                              scr["XDQT"][ts(dc, 128), T0:T0 + FFN_TP + 2])
                V("vector", lambda e: e.memset(mx[:, 0:NTL], 0.0), inc="ve")
                w1tok = [0, 0]
                w1tok[0] = 0
                for cch in range(NCH):
                    buf = cch % 2
                    W("gpsimd", "te", p.cnt["te"])
                    dvl = 0
                    for dc in range(ND):
                        dvl = dma("gpsimd",
                                  w1s[:, buf * ND * 128 + dc * 128:
                                      buf * ND * 128 + dc * 128 + 128],
                                  scr["W1QT"][ts(dc, 128), ts(cch, 128)])
                    W("tensor", "d_g", dvl)
                    W("tensor", "d_s", dvx)
                    bi = banks.acquire()
                    tv = None
                    for dc in range(ND):
                        last = dc == ND - 1
                        tv = V("tensor",
                               lambda e, dc=dc, bi=bi, buf=buf, f=(dc == 0),
                               l=last: nc.tensor.matmul(
                                   psb[bi][:, 0:FFN_TP + 2],
                                   w1s[:, buf * ND * 128 + dc * 128:
                                       buf * ND * 128 + dc * 128 + 128],
                                   xqp[:, ts(dc, FFN_TP + 2)],
                                   start=f, stop=l),
                               inc="te" if last else None)
                    W("scalar", "te", tv)
                    W("scalar", "ve", p.cnt["ve"])
                    sv = V("scalar", lambda e, bi=bi: e.activation(
                        hsb[:, :], psb[bi][:, 0:FFN_TP + 2], AF.Copy,
                        scale=colf[:, 40:41]), inc="sc")
                    banks.release(bi, "sc", sv)
                    W("vector", "sc", sv)
                    V("vector", lambda e, cch=cch: e.tensor_scalar(
                        hacc[:, :], hsb[:, 0:FFN_TP],
                        convc[:, cch * 4:cch * 4 + 1], None, ALU.mult),
                        inc="ve")
                    VSP()
                    V("vector", lambda e, cch=cch: e.scalar_tensor_tensor(
                        hacc[:, :], hsb[:, 1:FFN_TP + 1],
                        convc[:, cch * 4 + 1:cch * 4 + 2], hacc[:, :],
                        ALU.mult, ALU.add), inc="ve")
                    VSP()
                    vv = V("vector", lambda e, cch=cch:
                           e.scalar_tensor_tensor(
                               hacc[:, :], hsb[:, 2:FFN_TP + 2],
                               convc[:, cch * 4 + 2:cch * 4 + 3],
                               hacc[:, :], ALU.mult, ALU.add), inc="ve")
                    W("scalar", "ve", vv)
                    sv = V("scalar", lambda e, cch=cch: e.activation(
                        hact[:, ts(cch, FFN_TP)], hacc[:, :], AF.Silu,
                        bias=convc[:, cch * 4 + 3:cch * 4 + 4]), inc="sc")
                    W("vector", "sc", sv)
                    vv = V("vector", lambda e, cch=cch: e.tensor_copy(
                        hactb[:, :], hact[:, ts(cch, FFN_TP)]), inc="ve")
                    W("tensor", "ve", vv)
                    b = tp_state["rr"]
                    tp_state["rr"] = 1 - b
                    tok = tp_state["tok"][b]
                    if tok is not None:
                        p.wait("tensor", tok[0], tok[1])
                    tv = None
                    for tl in range(NTL):
                        tv = V("tensor", lambda e, b=b, tl=tl:
                               nc.tensor.transpose(
                                   psbT[b][:, ts(tl, 128)],
                                   hactb[:, ts(tl, 128)], eye[:, :]),
                               inc="te" if tl == NTL - 1 else None)
                    W("vector", "te", tv)
                    for tl in range(NTL):
                        V("vector", lambda e, b=b, tl=tl: e.tensor_reduce(
                            colf[:, 29:30], psbT[b][:, ts(tl, 128)], AX.X,
                            ALU.max, apply_absolute_value=True), inc="ve")
                        VSP()
                        vv = V("vector", lambda e, tl=tl: e.tensor_max(
                            mx[:, tl:tl + 1], mx[:, tl:tl + 1],
                            colf[:, 29:30]), inc="ve")
                    tp_state["tok"][b] = ("ve", vv)
                VSP()
                V("vector", lambda e: e.tensor_single_scalar(
                    mx[:, 0:NTL], mx[:, 0:NTL], 1e-5, ALU.max), inc="ve")
                VSP()
                V("vector", lambda e: e.reciprocal(
                    mx[:, 4:4 + NTL], mx[:, 0:NTL]), inc="ve")
                VSP()
                vv = V("vector", lambda e: e.tensor_single_scalar(
                    mx[:, 4:4 + NTL], mx[:, 4:4 + NTL], 127.0, ALU.mult),
                    inc="ve")
                vv = V("vector", lambda e: e.memset(junk[:, 0:8], 0.0),
                       inc="ve")
                vv = V("vector", lambda e: e.memset(junk[:, 0:8], 0.0),
                       inc="ve")
                W("tensor", "ve", vv)
                for tl in range(NTL):
                    bi_t = banks.acquire()
                    tvt = V("tensor", lambda e, bi=bi_t, tl=tl:
                            nc.tensor.transpose(
                                psb[bi][0:1, 0:128],
                                mx[:, 4 + tl:5 + tl], eyef[:, :]),
                            inc="te")
                    W("vector", "te", tvt)
                    vvt = V("vector", lambda e, bi=bi_t, tl=tl:
                            e.tensor_copy(s2r[0:1, ts(tl, 128)],
                                          psb[bi][0:1, 0:128]), inc="ve")
                    banks.release(bi_t, "ve", vvt)
                V("vector", lambda e: e.memset(junk[:, 0:8], 0.0), inc="ve")
                V("vector", lambda e: e.memset(junk[:, 0:8], 0.0), inc="ve")
                bcast_row(s2b[:, :], s2r[0:1, :], FFN_TP)
                VSP()
                for cch in range(NCH):
                    V("vector", lambda e, cch=cch: e.tensor_tensor(
                        hacc[:, :], hact[:, ts(cch, FFN_TP)], s2b[:, :],
                        ALU.mult), inc="ve")
                    V("vector", lambda e: e.tensor_single_scalar(
                        hacc[:, :], hacc[:, :], 128.5, ALU.add), inc="ve")
                    V("vector", lambda e, cch=cch: e.tensor_copy(
                        hact[:, ts(cch, FFN_TP)], hacc[:, :]), inc="ve")
                    hrem = hact[:, ts(cch, FFN_TP)]
                    for bit in (128.0, 64.0, 32.0, 16.0, 8.0, 4.0, 2.0,
                                1.0):
                        V("vector", lambda e, bit=bit, hrem=hrem:
                          e.tensor_single_scalar(
                              hsb[:, 0:FFN_TP], hrem, bit, ALU.is_ge),
                          inc="ve")
                        V("vector", lambda e, bit=bit, hrem=hrem:
                          e.scalar_tensor_tensor(
                              hrem, hsb[:, 0:FFN_TP], -bit, hrem,
                              ALU.mult, ALU.add), inc="ve")
                    V("vector", lambda e, hrem=hrem: e.tensor_sub(
                        hacc[:, :], hacc[:, :], hrem), inc="ve")
                    vv = V("vector", lambda e, cch=cch: e.tensor_single_scalar(
                        hq[:, ts(cch, FFN_TP)], hacc[:, :], -128.0,
                        ALU.add), inc="ve")
                W("tensor", "ve", vv)
                dbanks = {}
                tvs = {}
                for tl in range(NTL):
                    for ec2 in range(2):
                        dbanks[(tl, ec2)] = banks.acquire()
                te_after = []
                for cch in range(NCH):
                    buf = cch % 2
                    W("gpsimd", "te",
                      te_after[cch - 2] if cch >= 2 else p.cnt["te"])
                    w2t = dma("gpsimd", w2s[:, ts(buf, D)],
                              scr["W2QT"][ts(cch, 128), :])
                    W("tensor", "d_g", w2t)
                    for tl in range(NTL):
                        for ec2 in range(2):
                            last = cch == NCH - 1
                            fin = tl == NTL - 1 and ec2 == 1
                            tvs[(tl, ec2)] = V(
                                "tensor",
                                lambda e, cch=cch, tl=tl, ec2=ec2, buf=buf,
                                bi=dbanks[(tl, ec2)], f=(cch == 0), l=last:
                                nc.tensor.matmul(
                                    psb[bi][:, :],
                                    hq[:, cch * FFN_TP + tl * 128:
                                       cch * FFN_TP + tl * 128 + 128],
                                    w2s[:, buf * D + ec2 * 512:
                                        buf * D + ec2 * 512 + 512],
                                    start=f, stop=l),
                                inc="te" if (last or fin) else None)
                    te_after.append(p.cnt["te"])
                V("vector", lambda e: e.tensor_scalar(
                    mx[:, 6:6 + NTL], mx[:, 0:NTL], colf[:, 42:43],
                    1.0 / 127.0, ALU.mult, ALU.mult), inc="ve")
                VSP()
                for tl in range(NTL):
                    for ec2 in range(2):
                        W("vector", "te", tvs[(tl, ec2)])
                        ttg = tp * NTL + tl
                        rsl = res[:, ttg * D + ec2 * 512:
                                  ttg * D + ec2 * 512 + 512]
                        vv = V("vector",
                               lambda e, bi=dbanks[(tl, ec2)], rsl=rsl,
                               tl=tl: e.scalar_tensor_tensor(
                                   rsl, psb[bi][:, :],
                                   mx[:, 6 + tl:7 + tl], rsl,
                                   ALU.mult, ALU.add), inc="ve")
                        banks.release(dbanks[(tl, ec2)], "ve", vv)
                p.barrier()
        p.barrier()

    if stage >= 1:
        layernorm("XNT0", xq=False)
    if stage >= 2:
        l0_attention()
    if stage >= 3:
        l0_moe()
    if stage >= 4:
        layernorm("XNT1", xq=True)
    if stage >= 5:
        l1_attention()
    if stage >= 6:
        l1_ffn()

    for tt in range(NT):
        dma("sync", out[ts(tt, 128), :], res[:, ts(tt, D)])
    W("sync", "d_s", p.cnt["d_s"])
    p.build_block()
    p.stack.close()
    return nc, p


def kernel(**inputs):
    x = np.ascontiguousarray(inputs["L0_attn_x"], dtype=np.float32)
    nc = build_nc()
    shared = {k: np.ascontiguousarray(v) for k, v in inputs.items()
              if k != "L0_attn_x"}
    in_maps = []
    for c in range(8):
        m = dict(shared)
        m["x"] = np.ascontiguousarray(x[c])
        in_maps.append(m)
    r = run_bass_kernel_spmd(nc, in_maps, core_ids=list(range(8)))
    return np.stack([rr["out"] for rr in r.results], axis=0)
